# version 04
# baseline (speedup 1.0000x reference)
import numpy as np
import ml_dtypes
import concourse.bass as bass
import concourse.mybir as mybir
from concourse.bass_utils import run_bass_kernel_spmd

F32 = mybir.dt.float32
BF16 = mybir.dt.bfloat16
AF = mybir.ActivationFunctionType
ALU = mybir.AluOpType
AX = mybir.AxisListType

B, L, D, H, Dh = 4, 4096, 1024, 16, 64
E, DFF = 8, 4096
TPC = 2048
MT = 1024
NTC = MT // 128
NDC = D // 128
LN_EPS = 1e-5
TINY = 1.1754943508222875e-38
NLN2 = -0.6931471805599453

DBG = False
PHASES = frozenset(("boundary", "ln", "qvg", "scan", "norm", "wo", "ffn"))
LN_LIMIT = None
LN_SUB = None

ENGINES = ("pe", "act", "dve", "pool", "sp")
ENG_ATTR = {"pe": "tensor", "act": "scalar", "dve": "vector", "pool": "gpsimd", "sp": "sync"}
DMA_Q = {"pool": "qw", "sp": "qx", "act": "qo"}
NQSEM = 6


class _Op:
    __slots__ = ("stream", "emit", "deps", "q", "inc", "val", "waits", "post",
                 "seq", "token")


class Prog:

    def __init__(self):
        self.ops = []
        self.lw = {}
        self.lr = {}
        self.qcount = {"qw": 0, "qx": 0, "qo": 0}
        self.quse = {}
        self.qlast = {}

    def _add(self, stream, emit, r, w, q=None):
        op = _Op()
        op.stream, op.emit, op.q = stream, emit, q
        op.seq = len(self.ops)
        deps = set()
        for k in r:
            d = self.lw.get(k)
            if d is not None:
                deps.add(d)
        for k in w:
            d = self.lw.get(k)
            if d is not None:
                deps.add(d)
            for x in self.lr.get(k, ()):
                deps.add(x)
        for k in r:
            self.lr.setdefault(k, []).append(op)
        for k in w:
            self.lw[k] = op
            self.lr[k] = []
        if q is not None:
            qi = self.qcount[q] % NQSEM
            self.qcount[q] += 1
            sem_id = (q, qi)
            use = self.quse.get(sem_id, 0) + 1
            self.quse[sem_id] = use
            op.val = 16 * use
            op.token = (sem_id, op.val)
            prev = self.qlast.get(sem_id)
            if prev is not None:
                deps.add(prev)
            self.qlast[sem_id] = op
        deps.discard(op)
        op.deps = deps
        self.ops.append(op)
        return op

    def pe(self, emit, r=(), w=()):
        return self._add("pe", emit, r, w)

    def act(self, emit, r=(), w=()):
        return self._add("act", emit, r, w)

    def dve(self, emit, r=(), w=()):
        return self._add("dve", emit, r, w)

    def pool(self, emit, r=(), w=()):
        return self._add("pool", emit, r, w)

    def sp(self, emit, r=(), w=()):
        return self._add("sp", emit, r, w)

    def dma(self, eng, emit, r=(), w=()):
        return self._add(eng, emit, r, w, q=DMA_Q[eng])

    def finalize(self, block, sems):
        for op in self.ops:
            op.inc = op.q is not None
        for op in self.ops:
            for d in op.deps:
                if d.q is None and not (d.stream == "pe" and op.stream == "pe"):
                    d.inc = True
        counters = {k: 0 for k in ENGINES}
        for op in self.ops:
            if op.q is not None:
                pass
            elif op.inc:
                counters[op.stream] += 1
                op.val = counters[op.stream]
                op.token = (op.stream, op.val)
            else:
                op.val, op.token = None, None
        finals = dict(self.quse)
        clocks = {e: {} for e in ENGINES}
        for op in self.ops:
            clk = clocks[op.stream]
            waits = []
            for d in sorted(op.deps, key=lambda o: o.seq):
                if d.q is None and d.stream == "pe" and op.stream == "pe":
                    continue
                sem, val = d.token
                if clk.get(sem, 0) >= val:
                    continue
                waits.append((sem, val))
                for s2, v2 in d.post.items():
                    if clk.get(s2, 0) < v2:
                        clk[s2] = v2
                if clk.get(sem, 0) < val:
                    clk[sem] = val
            op.waits = waits
            post = dict(clk)
            if op.token is not None:
                s, v = op.token
                if post.get(s, 0) < v:
                    post[s] = v
            op.post = post

        def mksec(stream):
            def sec(eng):
                for op in self.ops:
                    if op.stream != stream:
                        continue
                    for sem_name, val in op.waits:
                        eng.wait_ge(sems[sem_name], val)
                    inst = op.emit(eng)
                    if op.q is not None:
                        inst.then_inc(sems[op.token[0]], 16)
                    elif op.inc:
                        inst.then_inc(sems[op.stream], 1)
                for sem_id, uses in finals.items():
                    eng.wait_ge(sems[sem_id], 16 * uses)
            return sec

        for stream in ENGINES:
            getattr(block, ENG_ATTR[stream])(mksec(stream))


def build():
    nc = bass.Bass("TRN2", target_bir_lowering=False)

    d_x = nc.dram_tensor("x", [TPC, D], F32, kind="ExternalInput").ap()
    d_xp = nc.dram_tensor("xp", [TPC, D], F32, kind="ExternalInput").ap()
    d_wT = {}
    for nm in ("wqT", "wvT", "wgT", "woT"):
        d_wT[nm] = nc.dram_tensor(nm, [D, D], F32, kind="ExternalInput").ap()
    d_gwT = nc.dram_tensor("gwT", [128, NDC, E], F32, kind="ExternalInput").ap()
    d_betaT = nc.dram_tensor("betaT", [128, NDC], F32, kind="ExternalInput").ap()
    d_gT = nc.dram_tensor("gT", [128, NDC], F32, kind="ExternalInput").ap()
    d_bT = nc.dram_tensor("bT", [128, NDC], F32, kind="ExternalInput").ap()
    d_w1 = nc.dram_tensor("w1", [E, D, DFF], F32, kind="ExternalInput").ap()
    d_b1T = nc.dram_tensor("b1T", [128, E, DFF // 128], F32, kind="ExternalInput").ap()
    d_w2 = nc.dram_tensor("w2", [E, DFF, D], F32, kind="ExternalInput").ap()
    d_b2 = nc.dram_tensor("b2", [E, D], F32, kind="ExternalInput").ap()
    d_ocb = nc.dram_tensor("ones_col_bf", [128, 1], BF16, kind="ExternalInput").ap()
    d_orb = nc.dram_tensor("ones_row_bf", [1, 128], BF16, kind="ExternalInput").ap()
    d_orf = nc.dram_tensor("ones_row_f", [1, 128], F32, kind="ExternalInput").ap()
    d_idf = nc.dram_tensor("iden_f", [128, 128], F32, kind="ExternalInput").ap()
    d_out = nc.dram_tensor("out", [TPC, D], F32, kind="ExternalOutput").ap()
    if DBG:
        d_dbg_xn = nc.dram_tensor("dbg_xn", [TPC, D], F32, kind="ExternalOutput").ap()
        d_dbg_mw = nc.dram_tensor("dbg_mw", [TPC, E], F32, kind="ExternalOutput").ap()
        d_dbg_attn = nc.dram_tensor("dbg_attn", [TPC, D], F32, kind="ExternalOutput").ap()
        d_dbg_c = {i: nc.dram_tensor(f"dbg_c{i}", [128, NDC, MT], F32,
                                     kind="ExternalOutput").ap() for i in (1, 2, 3, 5)}
        d_dbg_rows = nc.dram_tensor("dbg_rows", [3, MT], F32, kind="ExternalOutput").ap()
        d_dbg_acc0 = nc.dram_tensor("dbg_acc0", [128, D], F32, kind="ExternalOutput").ap()
        d_dbg_apsum = nc.dram_tensor("dbg_apsum", [128, D], F32, kind="ExternalOutput").ap()
        d_dbg_probe = nc.dram_tensor("dbg_probe", [6, 128, D], F32, kind="ExternalOutput").ap()

    from contextlib import ExitStack
    with ExitStack() as _stk:
        sb = lambda *a: _stk.enter_context(nc.sbuf_tensor(*a))
        ps = lambda *a: _stk.enter_context(nc.psum_tensor(*a))
        sm = lambda n: _stk.enter_context(nc.semaphore(n))
        c0 = sb("c0", [128, NDC, MT], BF16)
        c1 = sb("c1", [128, NDC, MT], BF16)
        c2 = sb("c2", [128, NDC, MT], BF16)
        c3 = sb("c3", [128, NDC, MT], BF16)
        c4 = sb("c4", [128, NDC, MT], BF16)
        c5 = sb("c5", [128, NDC, MT], BF16)
        w1b0 = sb("w1b0", [128, NDC, 512], BF16)
        w1b1 = sb("w1b1", [128, NDC, 512], BF16)
        xnt = sb("xnt", [128, NDC, MT], BF16)
        acc = sb("acc", [128, NTC, D], F32)
        xt0 = sb("xt0", [128, D], F32)
        xt1 = sb("xt1", [128, D], F32)
        st0 = sb("st0", [128, MT], F32)
        st1 = sb("st1", [128, MT], F32)
        r20 = sb("r20", [128, MT], BF16)
        r21 = sb("r21", [128, MT], BF16)
        xf = sb("xf", [128, 8, 128], F32)
        rowa = sb("rowa", [1, MT], F32)
        rowb = sb("rowb", [1, MT], F32)
        rowc = sb("rowc", [1, MT], F32)
        lncols = sb("lncols", [128, 8], F32)
        gcols = sb("gcols", [128, 8], F32)
        mw = sb("mw", [128, NTC, E], F32)
        lam = sb("lam", [128, NDC], F32)
        betas = sb("betas", [128, NDC], F32)
        gws = sb("gws", [128, NDC, E], F32)
        b1s = sb("b1s", [128, E, DFF // 128], F32)
        gts = sb("gts", [128, NDC], F32)
        bts = sb("bts", [128, NDC], F32)
        pstate = sb("pstate", [128, NDC], F32)
        b2s0 = sb("b2s0", [1, D], BF16)
        b2s1 = sb("b2s1", [1, D], BF16)
        ocb = sb("ocb", [128, 1], BF16)
        orb = sb("orb", [1, 128], BF16)
        orf = sb("orf", [1, 128], F32)
        idf = sb("idf", [128, 128], F32)
        epsc = sb("epsc", [128, 1], F32)
        pt0 = ps("pt0", [128, 1024], F32)
        pt1 = ps("pt1", [128, 1024], F32)
        pt2 = ps("pt2", [128, 1024], F32)
        pt3 = ps("pt3", [128, 1024], F32)
        sems = {}
        for e in ENGINES:
            sems[e] = sm(f"s_{e}")
        for qn in ("qw", "qx", "qo"):
            for i in range(NQSEM):
                sems[(qn, i)] = sm(f"s_{qn}{i}")
        block = _stk.enter_context(nc.Block())
        C = [c0, c1, c2, c3, c4, c5]
        W1B = [w1b0, w1b1]
        XT = [xt0, xt1]
        ST = [st0, st1]
        R2 = [r20, r21]
        B2S = [b2s0, b2s1]
        PT = [pt0, pt1, pt2, pt3]
        p = Prog()

        def pview(b, rows, cols, coloff=0):
            t, off = PT[b // 2], (b % 2) * 512
            return t[0:rows, off + coloff: off + coloff + cols]

        PBK = [("pb", i) for i in range(8)]

        def ck(ci, slot):
            return ("c", ci, slot)

        CALL = lambda ci: [ck(ci, s) for s in range(NDC)]

        def dbg_probe(k):
            if DBG:
                p.dma("act", lambda e, k=k: e.dma_start(
                    out=d_dbg_probe[k], in_=acc[:, 0, :]), r=[("acc", 0)])

        p.dma("sp", lambda e: e.dma_start(out=betas[:], in_=d_betaT[:]), w=[("betas",)])
        p.dma("sp", lambda e: e.dma_start(out=gws[:], in_=d_gwT[:]), w=[("gws",)])
        p.dma("sp", lambda e: e.dma_start(out=b1s[:], in_=d_b1T[:]), w=[("b1s",)])
        p.dma("sp", lambda e: e.dma_start(out=gts[:], in_=d_gT[:]), w=[("gts",)])
        p.dma("sp", lambda e: e.dma_start(out=bts[:], in_=d_bT[:]), w=[("bts",)])
        p.dma("sp", lambda e: e.dma_start(out=ocb[:], in_=d_ocb[:]), w=[("ocb",)])
        p.dma("sp", lambda e: e.dma_start(out=orb[:], in_=d_orb[:]), w=[("orb",)])
        p.dma("sp", lambda e: e.dma_start(out=orf[:], in_=d_orf[:]), w=[("orf",)])
        p.dma("sp", lambda e: e.dma_start(out=idf[:], in_=d_idf[:]), w=[("idf",)])
        p.act(lambda e: e.activation(lam[:], betas[:], AF.Exp, scale=NLN2),
              r=[("betas",)], w=[("lam",)])
        p.dve(lambda e: e.tensor_scalar(lam[:], lam[:], -1.0, 1.0, ALU.mult, ALU.add),
              r=[("lam",)], w=[("lam",)])
        p.dve(lambda e: e.tensor_scalar(lam[:], lam[:], TINY, 1.0, ALU.max, ALU.min),
              r=[("lam",)], w=[("lam",)])
        p.dve(lambda e: e.memset(pstate[:], 0.0),
              w=[("pst", cc) for cc in range(NDC)])
        p.dve(lambda e: e.memset(epsc[:], LN_EPS), w=[("epsc",)])

        tr_banks = (4, 5)
        gate_banks = (6, 7)

        def ln_chunk(x_dram, tci, par, boundary, mtv):
            xt = XT[par]
            kx = ("xt", par)
            tc = tci % NTC
            if LN_LIMIT is not None:
                if ln_ctr[0] >= LN_LIMIT:
                    return
                ln_ctr[0] += 1
            sub = 99 if LN_SUB is None else LN_SUB
            negmu = lncols[:, par: par + 1]
            sumsq = lncols[:, 2 + par: 3 + par]
            lnv = lncols[:, 4 + par: 5 + par]
            rstd = lncols[:, 6 + par: 7 + par]
            kc = ("lncols", par)
            p.dma("sp", lambda e: e.dma_start(
                out=xt[:], in_=x_dram[tci * 128:(tci + 1) * 128, :]), w=[kx])
            if sub < 2:
                return
            if not boundary:
                p.dve(lambda e: e.tensor_copy(acc[:, tc, :], xt[:]),
                      r=[kx], w=[("acc", tc)])
                if DBG and mtv == 0 and tc == 0:
                    p.dma("act", lambda e: e.dma_start(out=d_dbg_acc0[:], in_=acc[:, 0, :]),
                          r=[("acc", 0)])
            if sub < 3:
                return
            p.dve(lambda e: e.reduce_sum(negmu[:], xt[:], axis=AX.X, negate=True),
                  r=[kx], w=[kc])
            p.dve(lambda e: e.tensor_scalar(negmu[:], negmu[:], 1.0 / D, None, ALU.mult),
                  r=[kc], w=[kc])
            p.dve(lambda e: e.tensor_scalar(xt[:], xt[:], negmu[:], None, ALU.add),
                  r=[kx, kc], w=[kx])
            p.dve(lambda e: e.scalar_tensor_tensor(
                R2[par][:], xt[:], 1.0, xt[:], ALU.mult, ALU.mult, accum_out=sumsq[:]),
                r=[kx], w=[("r2", par), kc])
            if sub < 4:
                return
            p.act(lambda e: e.activation(lnv[:], sumsq[:], AF.Ln, bias=epsc[:], scale=1.0 / D),
                  r=[kc, ("epsc",)], w=[kc])
            p.act(lambda e: e.activation(rstd[:], lnv[:], AF.Exp, scale=-0.5),
                  r=[kc], w=[kc])
            p.act(lambda e: e.activation(xt[:], xt[:], AF.Identity, scale=rstd[:]),
                  r=[kx, kc], w=[kx])
            if DBG and not boundary:
                p.dma("act", lambda e: e.dma_start(
                    out=d_dbg_xn[(mtv * NTC + tc) * 128:(mtv * NTC + tc + 1) * 128, :],
                    in_=xt[:]), r=[kx])
            if sub < 5:
                return
            for dc in range(NDC):
                tb = tr_banks[dc % 2]
                tpv = pview(tb, 128, 128)
                p.pe(lambda e, dc=dc, tpv=tpv: e.transpose(
                    tpv, xt[:, dc * 128:(dc + 1) * 128], idf[:]),
                    r=[kx, ("idf",)], w=[PBK[tb]])
                if boundary:
                    p.act(lambda e, dc=dc, tpv=tpv, tc=tc: e.activation(
                        xnt[:, dc, tc * 128:(tc + 1) * 128], tpv, AF.Identity),
                        r=[PBK[tb]], w=[("xnt", dc, tc)])
                else:
                    p.act(lambda e, dc=dc, tpv=tpv: e.activation(
                        xf[:, dc, :], tpv, AF.Identity),
                        r=[PBK[tb]], w=[("xf", dc)])
                    p.dve(lambda e, dc=dc, tc=tc: e.tensor_copy(
                        xnt[:, dc, tc * 128:(tc + 1) * 128], xf[:, dc, :]),
                        r=[("xf", dc)], w=[("xnt", dc, tc)])
            if sub < 6 or boundary:
                return
            if True:
                gb = gate_banks[tc % 2]
                gpv = pview(gb, 128, E)
                for dc in range(NDC):
                    p.pe(lambda e, dc=dc, gpv=gpv: e.matmul(
                        gpv, xf[:, dc, :], gws[:, dc, :],
                        start=(dc == 0), stop=(dc == NDC - 1)),
                        r=[("xf", dc), ("gws",)], w=[PBK[gb]])
                gpar = tc % 2
                nmx = gcols[:, gpar: gpar + 1]
                mxc = gcols[:, 2 + gpar: 3 + gpar]
                den = gcols[:, 4 + gpar: 5 + gpar]
                rden = gcols[:, 6 + gpar: 7 + gpar]
                kg = ("gcols", gpar)
                kmw = ("mw", tc)
                p.dve(lambda e, gpv=gpv: e.reduce_max(nmx[:], gpv, axis=AX.X, negate=True),
                      r=[PBK[gb]], w=[kg])
                p.act(lambda e, gpv=gpv, tc=tc: e.activation(
                    mw[:, tc, :], gpv, AF.Exp, bias=nmx[:]),
                    r=[PBK[gb], kg], w=[kmw])
                p.dve(lambda e, tc=tc: e.reduce_sum(den[:], mw[:, tc, :], axis=AX.X),
                      r=[kmw], w=[kg])
                p.dve(lambda e: e.reciprocal(rden[:], den[:]), r=[kg], w=[kg])
                p.dve(lambda e: e.tensor_scalar(mxc[:], nmx[:], -1.0, None, ALU.mult),
                      r=[kg], w=[kg])
                p.dve(lambda e, gpv=gpv, tc=tc: e.tensor_scalar(
                    mw[:, tc, :], gpv, mxc[:], rden[:], ALU.is_ge, ALU.mult),
                    r=[PBK[gb], kg], w=[kmw])
                if DBG:
                    p.dma("act", lambda e, tc=tc: e.dma_start(
                        out=d_dbg_mw[(mtv * NTC + tc) * 128:(mtv * NTC + tc + 1) * 128, :],
                        in_=mw[:, tc, :]), r=[kmw])

        proj_ctr = [0]
        ln_ctr = [0]

        def projection(dsti, wbi, silu=False):
            dst = C[dsti]
            wbuf = C[wbi]
            for ic in range(NDC):
                for half in range(2):
                    bk = proj_ctr[0] % 4
                    proj_ctr[0] += 1
                    ppv = pview(bk, 128, 512)
                    rk0 = [ck(wbi, dc) for dc in range(NDC)] + \
                          [("xnt", dc, t) for dc in range(NDC)
                           for t in range(half * 4, half * 4 + 4)]
                    for dc in range(NDC):
                        p.pe(lambda e, ic=ic, half=half, dc=dc, ppv=ppv, wbuf=wbuf: e.matmul(
                            ppv, wbuf[:, dc, ic * 128:(ic + 1) * 128],
                            xnt[:, dc, half * 512:(half + 1) * 512],
                            start=(dc == 0), stop=(dc == NDC - 1)),
                            r=rk0 if dc == 0 else [ck(wbi, dc)], w=[PBK[bk]])
                    if silu:
                        p.act(lambda e, ic=ic, half=half, ppv=ppv, dst=dst: e.activation(
                            dst[:, ic, half * 512:(half + 1) * 512], ppv, AF.Silu),
                            r=[PBK[bk]], w=[ck(dsti, ic)])
                    else:
                        p.dve(lambda e, ic=ic, half=half, ppv=ppv, dst=dst: e.tensor_copy(
                            dst[:, ic, half * 512:(half + 1) * 512], ppv),
                            r=[PBK[bk]], w=[ck(dsti, ic)])

        def w_dma(dsti, src):
            p.dma("pool", lambda e, dsti=dsti: e.dma_start(
                out=C[dsti][:], in_=src.rearrange("(dc p) i -> p dc i", p=128)),
                w=CALL(dsti))

        def scan_chunk(cc, par, vti):
            ks = ("st", par)
            p.dve(lambda e, cc=cc, par=par, vti=vti: e.tensor_tensor_scan(
                ST[par][:], lam[:, cc:cc + 1].broadcast_to([128, MT]),
                C[vti][:, cc, :], pstate[:, cc:cc + 1], ALU.mult, ALU.add),
                r=[ck(vti, cc), ("lam",), ("pst", cc)], w=[ks])
            p.dve(lambda e, cc=cc, par=par: e.tensor_copy(
                pstate[:, cc:cc + 1], ST[par][:, MT - 1:MT]),
                r=[ks], w=[("pst", cc)])

        if "boundary" in PHASES:
            w_dma(4, d_wT["wvT"])
            for pmt in range(2):
                for tc in range(NTC):
                    ln_chunk(d_xp, pmt * NTC + tc, tc % 2, True, 0)
                projection(1, 4)
                for cc in range(NDC):
                    scan_chunk(cc, cc % 2, 1)

        for mtv in range(2):
            mtbase = mtv * MT
            if "ln" not in PHASES:
                break

            for tc in range(NTC):
                ln_chunk(d_x, mtv * NTC + tc, tc % 2, False, mtv)

            if mtv == 0:
                dbg_probe(0)
            if "qvg" not in PHASES:
                continue
            if mtv == 1 or "boundary" not in PHASES:
                w_dma(4, d_wT["wvT"])
            projection(1, 4)
            w_dma(5, d_wT["wqT"])
            projection(0, 5)
            w_dma(4, d_wT["wgT"])
            projection(2, 4, silu=True)
            w_dma(5, d_wT["woT"])
            if mtv == 0:
                dbg_probe(1)
            if "scan" not in PHASES:
                continue

            for cc in range(NDC):
                par = cc % 2
                scan_chunk(cc, par, 1)
                p.dve(lambda e, cc=cc, par=par: e.tensor_tensor(
                    C[1][:, cc, :], ST[par][:], C[0][:, cc, :], ALU.mult),
                    r=[("st", par), ck(0, cc)], w=[ck(1, cc)])
                p.dve(lambda e, cc=cc, par=par: e.scalar_tensor_tensor(
                    R2[par][:], C[1][:, cc, :], 1.0, C[1][:, cc, :], ALU.mult, ALU.mult),
                    r=[ck(1, cc)], w=[("r2", par)])
                for jt in range(2):
                    mb = 4 + jt
                    sb = 6 + jt
                    p.pe(lambda e, cc=cc, jt=jt, mb=mb: e.matmul(
                        pview(mb, 1, 512), ocb[:], C[1][:, cc, jt * 512:(jt + 1) * 512],
                        start=(cc == 0), stop=(cc == NDC - 1)),
                        r=[ck(1, cc), ("ocb",)], w=[PBK[mb]])
                    p.pe(lambda e, cc=cc, jt=jt, par=par, sb=sb: e.matmul(
                        pview(sb, 1, 512), ocb[:], R2[par][:, jt * 512:(jt + 1) * 512],
                        start=(cc == 0), stop=(cc == NDC - 1)),
                        r=[("r2", par), ("ocb",)], w=[PBK[sb]])

            if mtv == 0:
                dbg_probe(2)
            if "norm" not in PHASES:
                continue
            for jt in range(2):
                mb, sb = 4 + jt, 6 + jt
                ra = rowa[0:1, jt * 512:(jt + 1) * 512]
                rb = rowb[0:1, jt * 512:(jt + 1) * 512]
                rc = rowc[0:1, jt * 512:(jt + 1) * 512]
                p.dve(lambda e, mb=mb, ra=ra: e.tensor_scalar(
                    ra, pview(mb, 1, 512), -1.0 / D, None, ALU.mult),
                    r=[PBK[mb]], w=[("rowa", jt)])
                p.dve(lambda e, ra=ra, rc=rc: e.tensor_tensor(rc, ra, ra, ALU.mult),
                      r=[("rowa", jt)], w=[("rowc", jt)])
                p.dve(lambda e, sb=sb, rc=rc: e.scalar_tensor_tensor(
                    rc, pview(sb, 1, 512), 1.0 / D, rc, ALU.mult, ALU.subtract),
                    r=[PBK[sb], ("rowc", jt)], w=[("rowc", jt)])
                p.act(lambda e, rb=rb, rc=rc: e.activation(rb, rc, AF.Ln, bias=epsc[0:1, 0:1]),
                      r=[("rowc", jt), ("epsc",)], w=[("rowb", jt)])
                p.act(lambda e, rb=rb, rc=rc: e.activation(rc, rb, AF.Exp, scale=-0.5),
                      r=[("rowb", jt)], w=[("rowc", jt)])
                p.pe(lambda e, jt=jt, ra=ra: e.matmul(
                    pview(4 + jt, 128, 512), orf[:], ra, start=True, stop=True),
                    r=[("rowa", jt), ("orf",)], w=[PBK[4 + jt]])
                p.pe(lambda e, jt=jt, rc=rc: e.matmul(
                    pview(6 + jt, 128, 512), orf[:], rc, start=True, stop=True),
                    r=[("rowc", jt), ("orf",)], w=[PBK[6 + jt]])

            for cc in range(NDC):
                par = cc % 2
                sv = ST[par]
                ks = ("st", par)
                p.dve(lambda e, cc=cc, sv=sv: e.tensor_tensor(
                    sv[:], C[1][:, cc, :], pt2[:, :], ALU.add),
                    r=[ck(1, cc), PBK[4], PBK[5]], w=[ks])
                p.dve(lambda e, sv=sv: e.tensor_tensor(sv[:], sv[:], pt3[:, :], ALU.mult),
                      r=[ks, PBK[6], PBK[7]], w=[ks])
                p.dve(lambda e, cc=cc, sv=sv: e.tensor_scalar(
                    sv[:], sv[:], gts[:, cc:cc + 1], bts[:, cc:cc + 1], ALU.mult, ALU.add),
                    r=[ks, ("gts",), ("bts",)], w=[ks])
                p.dve(lambda e, cc=cc, sv=sv: e.tensor_tensor(
                    C[3][:, cc, :], sv[:], C[2][:, cc, :], ALU.mult),
                    r=[ks, ck(2, cc)], w=[ck(3, cc)])

            if mtv == 0:
                dbg_probe(3)
            if "wo" not in PHASES:
                continue
            for tc in range(NTC):
                ybank = (tc % 2) * 2
                ypt = PT[tc % 2]
                ybk = [PBK[ybank], PBK[ybank + 1]]
                for jt in range(2):
                    for ic in range(NDC):
                        p.pe(lambda e, tc=tc, jt=jt, ic=ic, ypt=ypt: e.matmul(
                            ypt[:, jt * 512:(jt + 1) * 512],
                            C[3][:, ic, tc * 128:(tc + 1) * 128],
                            C[5][:, ic, jt * 512:(jt + 1) * 512],
                            start=(ic == 0), stop=(ic == NDC - 1)),
                            r=[ck(3, ic), ck(5, ic)], w=[ybk[jt]])
                if DBG and mtv == 0 and tc == 0:
                    p.dve(lambda e, ypt=ypt: e.tensor_copy(st0[:], ypt[:, :]),
                          r=ybk + [("st", 0)], w=[("st", 0)])
                    p.dma("act", lambda e: e.dma_start(out=d_dbg_apsum[:], in_=st0[:]),
                          r=[("st", 0)])
                p.dve(lambda e, tc=tc, ypt=ypt: e.scalar_tensor_tensor(
                    acc[:, tc, :], ypt[:, :], 1.0, acc[:, tc, :], ALU.mult, ALU.add),
                    r=ybk + [("acc", tc)], w=[("acc", tc)])
                if DBG:
                    p.dma("act", lambda e, tc=tc, mtbase=mtbase: e.dma_start(
                        out=d_dbg_attn[(mtbase + tc * 128):(mtbase + (tc + 1) * 128), :],
                        in_=acc[:, tc, :]), r=[("acc", tc)])

            if DBG and mtv == 0:
                for i in (1, 2, 3, 5):
                    p.dma("pool", lambda e, i=i: e.dma_start(
                        out=d_dbg_c[i][:], in_=C[i][:]),
                        r=[ck(i, s) for s in range(NDC)])
                p.dma("pool", lambda e: e.dma_start(out=d_dbg_rows[0:1, :], in_=rowa[:]),
                      r=[("rowa", 0), ("rowa", 1)])
                p.dma("pool", lambda e: e.dma_start(out=d_dbg_rows[1:2, :], in_=rowb[:]),
                      r=[("rowb", 0), ("rowb", 1)])
                p.dma("pool", lambda e: e.dma_start(out=d_dbg_rows[2:3, :], in_=rowc[:]),
                      r=[("rowc", 0), ("rowc", 1)])

            if "ffn" not in PHASES:
                for tc in range(NTC):
                    p.dma("act", lambda e, tc=tc, mtbase=mtbase: e.dma_start(
                        out=d_out[mtbase + tc * 128: mtbase + (tc + 1) * 128, :],
                        in_=acc[:, tc, :]), r=[("acc", tc)])
                continue
            stages = [(ei, h) for ei in range(E) for h in range(2)]
            w18ctr = [mtv * 64]
            stage_w1 = {}

            def stage_w2_dmas(si):
                ei, h = stages[si]
                if h == 0:
                    b2par = ei % 2
                    p.dma("pool", lambda e, ei=ei, b2par=b2par: e.dma_start(
                        out=B2S[b2par][:], in_=d_b2[ei:ei + 1, :]), w=[("b2s", b2par)])
                w2pair = (2, 3) if si % 2 == 0 else (4, 5)
                for ci in range(2):
                    cdst = C[w2pair[ci]]
                    p.dma("pool", lambda e, ei=ei, h=h, ci=ci, cdst=cdst: e.dma_start(
                        out=cdst[:],
                        in_=d_w2[ei][h * 2048 + ci * 1024: h * 2048 + (ci + 1) * 1024, :]
                        .rearrange("(fc p) c -> p fc c", p=128)),
                        w=CALL(w2pair[ci]))

            def stage_w1_dmas(si, ks):
                ei, h = stages[si]
                lst = stage_w1.setdefault(si, {})
                for k8 in ks:
                    wpar = w18ctr[0] % 2
                    w18ctr[0] += 1
                    wb = W1B[wpar]
                    wbk = ("w1b", wpar)
                    f0 = h * 2048 + k8 * 512
                    p.dma("pool", lambda e, ei=ei, f0=f0, wb=wb: e.dma_start(
                        out=wb[:],
                        in_=d_w1[ei][:, f0:f0 + 512].rearrange("(dc p) f -> p dc f", p=128)),
                        w=[wbk])
                    lst[k8] = (wb, wbk)

            stage_w2_dmas(0)
            stage_w1_dmas(0, (0, 1))
            for si in range(len(stages)):
                if si + 1 < len(stages):
                    stage_w2_dmas(si + 1)
                ei, h = stages[si]
                b2par = ei % 2
                w2pair = (2, 3) if si % 2 == 0 else (4, 5)
                if True:
                    for k8 in range(4):
                        if k8 == 2:
                            stage_w1_dmas(si, (2, 3))
                        wb, wbk = stage_w1[si][k8]
                        for fcl in range(4):
                            fch = k8 * 4 + fcl
                            hdi = fch // 8
                            hsl = fch % 8
                            fc32 = h * 16 + fch
                            for tt in range(2):
                                hb = 4 + (fcl * 2 + tt) % 2
                                hpv = pview(hb, 128, 512)
                                for dc in range(NDC):
                                    p.pe(lambda e, fcl=fcl, tt=tt, dc=dc, wb=wb,
                                         hpv=hpv: e.matmul(
                                             hpv, wb[:, dc, fcl * 128:(fcl + 1) * 128],
                                             xnt[:, dc, tt * 512:(tt + 1) * 512],
                                             start=(dc == 0), stop=(dc == NDC - 1)),
                                        r=[wbk] + ([("xnt", dc2, t) for dc2 in range(NDC)
                                                    for t in range(tt * 4, tt * 4 + 4)]
                                                   if dc == 0 else [wbk]),
                                        w=[PBK[hb]])
                                p.act(lambda e, hdi=hdi, hsl=hsl, tt=tt, hpv=hpv,
                                      ei=ei, fc32=fc32: e.activation(
                                          C[hdi][:, hsl, tt * 512:(tt + 1) * 512], hpv,
                                          AF.Relu, bias=b1s[:, ei, fc32:fc32 + 1]),
                                      r=[PBK[hb], ("b1s",)], w=[ck(hdi, hsl)])
                    if si + 1 < len(stages):
                        stage_w1_dmas(si + 1, (0, 1))
                    for tsub in range(NTC):
                        ybank = (tsub % 2) * 2
                        ypt = PT[tsub % 2]
                        ybk = [PBK[ybank], PBK[ybank + 1]]
                        for jt in range(2):
                            p.pe(lambda e, jt=jt, ypt=ypt, b2par=b2par: e.matmul(
                                ypt[:, jt * 512:(jt + 1) * 512], orb[:],
                                B2S[b2par][0:1, jt * 512:(jt + 1) * 512],
                                start=True, stop=False),
                                r=[("b2s", b2par), ("orb",)], w=[ybk[jt]])
                        for fch in range(16):
                            hdi = fch // 8
                            hsl = fch % 8
                            w2i = w2pair[fch // 8]
                            for jt in range(2):
                                p.pe(lambda e, fch=fch, hdi=hdi, hsl=hsl, jt=jt,
                                     tsub=tsub, ypt=ypt, w2i=w2i: e.matmul(
                                         ypt[:, jt * 512:(jt + 1) * 512],
                                         C[hdi][:, hsl, tsub * 128:(tsub + 1) * 128],
                                         C[w2i][:, fch % 8, jt * 512:(jt + 1) * 512],
                                         start=False, stop=(fch == 15)),
                                    r=[ck(hdi, hsl), ck(w2i, fch % 8)],
                                    w=[ybk[jt]])
                        p.dve(lambda e, tsub=tsub, ypt=ypt, ei=ei: e.scalar_tensor_tensor(
                            acc[:, tsub, :], ypt[:, :], mw[:, tsub, ei:ei + 1],
                            acc[:, tsub, :], ALU.mult, ALU.add),
                            r=ybk + [("mw", tsub), ("acc", tsub)], w=[("acc", tsub)])
                        if ei == E - 1 and h == 1:
                            p.dma("act", lambda e, tsub=tsub, mtbase=mtbase: e.dma_start(
                                out=d_out[mtbase + tsub * 128: mtbase + (tsub + 1) * 128, :],
                                in_=acc[:, tsub, :]), r=[("acc", tsub)])

        p.finalize(block, sems)
    return nc


def kernel(L0_attn_x, L0_attn_Wq, L0_attn_Wv, L0_attn_Wo, L0_attn_Wg,
           L0_attn_beta, L0_attn_ln_gamma, L0_attn_ln_beta,
           L0_ffn_gate_w, L0_ffn_w1, L0_ffn_b1, L0_ffn_w2, L0_ffn_b2):
    f32 = np.float32
    x = np.asarray(L0_attn_x, dtype=f32)
    common = {
        "wqT": np.ascontiguousarray(np.asarray(L0_attn_Wq, f32).T),
        "wvT": np.ascontiguousarray(np.asarray(L0_attn_Wv, f32).T),
        "wgT": np.ascontiguousarray(np.asarray(L0_attn_Wg, f32).T),
        "woT": np.ascontiguousarray(np.asarray(L0_attn_Wo, f32).T),
        "gwT": np.ascontiguousarray(
            np.asarray(L0_ffn_gate_w, f32).T.reshape(NDC, 128, E).transpose(1, 0, 2)),
        "betaT": np.ascontiguousarray(
            np.repeat(np.asarray(L0_attn_beta, f32), Dh).reshape(NDC, 128).T),
        "gT": np.ascontiguousarray(np.asarray(L0_attn_ln_gamma, f32).reshape(NDC, 128).T),
        "bT": np.ascontiguousarray(np.asarray(L0_attn_ln_beta, f32).reshape(NDC, 128).T),
        "w1": np.ascontiguousarray(np.asarray(L0_ffn_w1, f32)),
        "b1T": np.ascontiguousarray(
            np.asarray(L0_ffn_b1, f32).reshape(E, DFF // 128, 128).transpose(2, 0, 1)),
        "w2": np.ascontiguousarray(np.asarray(L0_ffn_w2, f32)),
        "b2": np.ascontiguousarray(np.asarray(L0_ffn_b2, f32)),
        "ones_col_bf": np.ones((128, 1), dtype=ml_dtypes.bfloat16),
        "ones_row_bf": np.ones((1, 128), dtype=ml_dtypes.bfloat16),
        "ones_row_f": np.ones((1, 128), dtype=f32),
        "iden_f": np.eye(128, dtype=f32),
    }
    zeros = np.zeros((TPC, D), dtype=f32)
    in_maps = []
    for core in range(8):
        b, half = core // 2, core % 2
        m = dict(common)
        m["x"] = np.ascontiguousarray(x[b, half * TPC:(half + 1) * TPC, :])
        m["xp"] = zeros if half == 0 else np.ascontiguousarray(x[b, 0:TPC, :])
        in_maps.append(m)

    nc = build()
    res = run_bass_kernel_spmd(nc, in_maps, core_ids=list(range(8)))
    out = np.empty((B, L, D), dtype=f32)
    for core in range(8):
        b, half = core // 2, core % 2
        out[b, half * TPC:(half + 1) * TPC, :] = res.results[core]["out"]
    if DBG:
        kernel.dbg = res.results
    return out


# version 07
# speedup vs baseline: 1.3160x; 1.3160x over previous
import numpy as np
import ml_dtypes
import concourse.bass as bass
import concourse.mybir as mybir
from concourse.bass_utils import run_bass_kernel_spmd
from concourse.bass import IndirectOffsetOnAxis

F32 = mybir.dt.float32
BF16 = mybir.dt.bfloat16
I32 = mybir.dt.int32
AF = mybir.ActivationFunctionType
ALU = mybir.AluOpType
AX = mybir.AxisListType

B, L, D, H, Dh = 4, 4096, 1024, 16, 64
E, DFF = 8, 4096
TPC = 2048
MT = 1024
NTC = MT // 128
NDC = D // 128
LN_EPS = 1e-5
TINY = 1.1754943508222875e-38
NLN2 = -0.6931471805599453

DBG = False
PHASES = frozenset(("boundary", "ln", "qvg", "scan", "norm", "wo", "ffn"))
CAP = 384
NGC = CAP // 128
PADIDX = 20000
BIGPOS = 100000.0
LN_LIMIT = None
NO_SCATTER = False
SCATTER_OP = "add"
FFN_STAGES = None
LN_SUB = None

ENGINES = ("pe", "act", "dve", "pool", "sp")
ENG_ATTR = {"pe": "tensor", "act": "scalar", "dve": "vector", "pool": "gpsimd", "sp": "sync"}
DMA_Q = {"pool": "qw", "sp": "qx", "act": "qo"}
NQSEM = 6


class _Op:
    __slots__ = ("stream", "emit", "deps", "q", "inc", "val", "waits", "post",
                 "seq", "token")


class Prog:

    def __init__(self):
        self.pool_prologue = None
        self.ops = []
        self.lw = {}
        self.lr = {}
        self.qcount = {"qw": 0, "qx": 0, "qo": 0}
        self.quse = {}
        self.qlast = {}

    def _add(self, stream, emit, r, w, q=None):
        op = _Op()
        op.stream, op.emit, op.q = stream, emit, q
        op.seq = len(self.ops)
        deps = set()
        for k in r:
            d = self.lw.get(k)
            if d is not None:
                deps.add(d)
        for k in w:
            d = self.lw.get(k)
            if d is not None:
                deps.add(d)
            for x in self.lr.get(k, ()):
                deps.add(x)
        for k in r:
            self.lr.setdefault(k, []).append(op)
        for k in w:
            self.lw[k] = op
            self.lr[k] = []
        if q is not None:
            qi = self.qcount[q] % NQSEM
            self.qcount[q] += 1
            sem_id = (q, qi)
            use = self.quse.get(sem_id, 0) + 1
            self.quse[sem_id] = use
            op.val = 16 * use
            op.token = (sem_id, op.val)
            prev = self.qlast.get(sem_id)
            if prev is not None:
                deps.add(prev)
            self.qlast[sem_id] = op
        deps.discard(op)
        op.deps = deps
        self.ops.append(op)
        return op

    def pe(self, emit, r=(), w=()):
        return self._add("pe", emit, r, w)

    def act(self, emit, r=(), w=()):
        return self._add("act", emit, r, w)

    def dve(self, emit, r=(), w=()):
        return self._add("dve", emit, r, w)

    def pool(self, emit, r=(), w=()):
        return self._add("pool", emit, r, w)

    def sp(self, emit, r=(), w=()):
        return self._add("sp", emit, r, w)

    def dma(self, eng, emit, r=(), w=()):
        return self._add(eng, emit, r, w, q=DMA_Q[eng])

    def finalize(self, block, sems):
        for op in self.ops:
            op.inc = op.q is not None
        for op in self.ops:
            for d in op.deps:
                if d.q is None and not (d.stream == "pe" and op.stream == "pe"):
                    d.inc = True
        counters = {k: 0 for k in ENGINES}
        for op in self.ops:
            if op.q is not None:
                pass
            elif op.inc:
                counters[op.stream] += 1
                op.val = counters[op.stream]
                op.token = (op.stream, op.val)
            else:
                op.val, op.token = None, None
        finals = dict(self.quse)
        clocks = {e: {} for e in ENGINES}
        for op in self.ops:
            clk = clocks[op.stream]
            waits = []
            for d in sorted(op.deps, key=lambda o: o.seq):
                if d.q is None and d.stream == "pe" and op.stream == "pe":
                    continue
                sem, val = d.token
                if clk.get(sem, 0) >= val:
                    continue
                waits.append((sem, val))
                for s2, v2 in d.post.items():
                    if clk.get(s2, 0) < v2:
                        clk[s2] = v2
                if clk.get(sem, 0) < val:
                    clk[sem] = val
            op.waits = waits
            post = dict(clk)
            if op.token is not None:
                s, v = op.token
                if post.get(s, 0) < v:
                    post[s] = v
            op.post = post

        def mksec(stream):
            def sec(eng):
                if stream == "pool" and self.pool_prologue is not None:
                    self.pool_prologue(eng)
                for op in self.ops:
                    if op.stream != stream:
                        continue
                    for sem_name, val in op.waits:
                        eng.wait_ge(sems[sem_name], val)
                    try:
                        inst = op.emit(eng)
                    except Exception as ex:
                        raise RuntimeError(f"emit failed seq={op.seq} stream={op.stream}: {ex}") from ex
                    if op.q is not None:
                        inst.then_inc(sems[op.token[0]], 16)
                    elif op.inc:
                        inst.then_inc(sems[op.stream], 1)
                for sem_id, uses in finals.items():
                    eng.wait_ge(sems[sem_id], 16 * uses)
            return sec

        for stream in ENGINES:
            getattr(block, ENG_ATTR[stream])(mksec(stream))


def build():
    nc = bass.Bass("TRN2", target_bir_lowering=False)

    d_x = nc.dram_tensor("x", [TPC, D], F32, kind="ExternalInput").ap()
    d_xp = nc.dram_tensor("xp", [TPC, D], F32, kind="ExternalInput").ap()
    d_wT = {}
    for nm in ("wqT", "wvT", "wgT", "woT"):
        d_wT[nm] = nc.dram_tensor(nm, [D, D], F32, kind="ExternalInput").ap()
    d_gwT = nc.dram_tensor("gwT", [128, NDC, E], F32, kind="ExternalInput").ap()
    d_betaT = nc.dram_tensor("betaT", [128, NDC], F32, kind="ExternalInput").ap()
    d_gT = nc.dram_tensor("gT", [128, NDC], F32, kind="ExternalInput").ap()
    d_bT = nc.dram_tensor("bT", [128, NDC], F32, kind="ExternalInput").ap()
    d_w1 = nc.dram_tensor("w1", [E, D, DFF], F32, kind="ExternalInput").ap()
    d_b1T = nc.dram_tensor("b1T", [128, E, DFF // 128], F32, kind="ExternalInput").ap()
    d_w2 = nc.dram_tensor("w2", [E, DFF, D], F32, kind="ExternalInput").ap()
    d_b2 = nc.dram_tensor("b2", [E, D], F32, kind="ExternalInput").ap()
    d_ocb = nc.dram_tensor("ones_col_bf", [128, 1], BF16, kind="ExternalInput").ap()
    d_orb = nc.dram_tensor("ones_row_bf", [1, 128], BF16, kind="ExternalInput").ap()
    d_orf = nc.dram_tensor("ones_row_f", [1, 128], F32, kind="ExternalInput").ap()
    d_idf = nc.dram_tensor("iden_f", [128, 128], F32, kind="ExternalInput").ap()
    d_tri = nc.dram_tensor("triS", [128, 128], BF16, kind="ExternalInput").ap()
    d_iotaI = nc.dram_tensor("iotaI", [128, NTC], I32, kind="ExternalInput").ap()
    d_padrow = nc.dram_tensor("padrow", [1, CAP], I32, kind="ExternalInput").ap()
    d_xnd = nc.dram_tensor("xnd", [MT, D], F32, kind="Internal").ap()
    d_mwd = nc.dram_tensor("mwd", [MT, E], F32, kind="Internal").ap()
    d_gl = nc.dram_tensor("glist", [E, CAP], I32, kind="Internal").ap()
    d_out = nc.dram_tensor("out", [TPC, D], F32, kind="ExternalOutput").ap()
    if DBG:
        d_dbg_xn = nc.dram_tensor("dbg_xn", [TPC, D], F32, kind="ExternalOutput").ap()
        d_dbg_mw = nc.dram_tensor("dbg_mw", [TPC, E], F32, kind="ExternalOutput").ap()
        d_dbg_attn = nc.dram_tensor("dbg_attn", [TPC, D], F32, kind="ExternalOutput").ap()
        d_dbg_c = {i: nc.dram_tensor(f"dbg_c{i}", [128, NDC, MT], F32,
                                     kind="ExternalOutput").ap() for i in (1, 2, 3, 5)}
        d_dbg_rows = nc.dram_tensor("dbg_rows", [3, MT], F32, kind="ExternalOutput").ap()
        d_dbg_acc0 = nc.dram_tensor("dbg_acc0", [128, D], F32, kind="ExternalOutput").ap()
        d_dbg_apsum = nc.dram_tensor("dbg_apsum", [128, D], F32, kind="ExternalOutput").ap()
        d_dbg_probe = nc.dram_tensor("dbg_probe", [6, 128, D], F32, kind="ExternalOutput").ap()

    from contextlib import ExitStack
    with ExitStack() as _stk:
        sb = lambda *a: _stk.enter_context(nc.sbuf_tensor(*a))
        ps = lambda *a: _stk.enter_context(nc.psum_tensor(*a))
        sm = lambda n: _stk.enter_context(nc.semaphore(n))
        c0 = sb("c0", [128, NDC, MT], BF16)
        c1 = sb("c1", [128, NDC, MT], BF16)
        c2 = sb("c2", [128, NDC, MT], BF16)
        c3 = sb("c3", [128, NDC, MT], BF16)
        c4 = sb("c4", [128, NDC, MT], BF16)
        c5 = sb("c5", [128, NDC, MT], BF16)
        w1b0 = sb("w1b0", [128, NDC, 512], BF16)
        w1b1 = sb("w1b1", [128, NDC, 512], BF16)
        xnt = sb("xnt", [128, NDC, MT], BF16)
        xt0 = sb("xt0", [128, D], F32)
        xt1 = sb("xt1", [128, D], F32)
        st0 = sb("st0", [128, MT], F32)
        st1 = sb("st1", [128, MT], F32)
        r20 = sb("r20", [128, MT], BF16)
        r21 = sb("r21", [128, MT], BF16)
        xf = sb("xf", [128, 8, 128], F32)
        rowa = sb("rowa", [1, MT], F32)
        rowb = sb("rowb", [1, MT], F32)

        lncols = sb("lncols", [128, 8], F32)
        gcols = sb("gcols", [128, 8], F32)
        mw = sb("mw", [128, NTC, E], F32)
        lam = sb("lam", [128, NDC], F32)
        betas = sb("betas", [128, NDC], F32)
        gws = sb("gws", [128, NDC, E], F32)
        b1s = sb("b1s", [128, E, DFF // 128], F32)
        gts = sb("gts", [128, NDC], F32)
        bts = sb("bts", [128, NDC], F32)
        pstate = sb("pstate", [128, NDC], F32)
        b2s0 = sb("b2s0", [1, D], BF16)
        b2s1 = sb("b2s1", [1, D], BF16)
        ocb = sb("ocb", [128, 1], BF16)
        orb = sb("orb", [1, 128], BF16)
        orf = sb("orf", [1, 128], F32)
        idf = sb("idf", [128, 128], F32)
        epsc = sb("epsc", [128, 1], F32)
        xgt = [sb("xgt0", [128, NDC, CAP], BF16), sb("xgt1", [128, NDC, CAP], BF16)]
        idxc = [sb("idxc0", [128, NGC], I32), sb("idxc1", [128, NGC], I32)]
        mgS = [sb("mg0", [128, NGC, E], F32), sb("mg1", [128, NGC, E], F32)]
        mkf = sb("mkf", [128, NTC], F32)
        mkb = sb("mkb", [128, NTC], BF16)
        posw = sb("posw", [128, NTC], F32)
        t1w = sb("t1w", [128, NTC], F32)
        posi = sb("posi", [128, NTC], I32)
        crow = sb("crow", [1, 2 * NTC], F32)
        triSs = sb("triSs", [128, 128], BF16)
        padr = sb("padr", [1, CAP], I32)
        ytb = [sb("ytb0", [128, D], F32), sb("ytb1", [128, D], F32)]
        iotas = sb("iotas", [128, NTC], I32)
        pt0 = ps("pt0", [128, 1024], F32)
        pt1 = ps("pt1", [128, 1024], F32)
        pt2 = ps("pt2", [128, 1024], F32)
        pt3 = ps("pt3", [128, 1024], F32)
        sems = {}
        for e in ENGINES:
            sems[e] = sm(f"s_{e}")
        for qn in ("qw", "qx", "qo"):
            for i in range(NQSEM):
                sems[(qn, i)] = sm(f"s_{qn}{i}")
        block = _stk.enter_context(nc.Block())
        C = [c0, c1, c2, c3, c4, c5]
        W1B = [w1b0, w1b1]
        XT = [xt0, xt1]
        ST = [st0, st1]
        R2 = [r20, r21]
        B2S = [b2s0, b2s1]
        PT = [pt0, pt1, pt2, pt3]
        p = Prog()
        POOLREGS = {}

        def _pl(eng):
            POOLREGS["cap"] = eng.to_reg(CAP - 1)
            POOLREGS["mt"] = eng.to_reg(MT - 1)
        p.pool_prologue = _pl

        def pview(b, rows, cols, coloff=0):
            t, off = PT[b // 2], (b % 2) * 512
            return t[0:rows, off + coloff: off + coloff + cols]

        PBK = [("pb", i) for i in range(8)]

        def ck(ci, slot):
            return ("c", ci, slot)

        CALL = lambda ci: [ck(ci, s) for s in range(NDC)]

        def dbg_probe(k):
            pass

        p.dma("sp", lambda e: e.dma_start(out=betas[:], in_=d_betaT[:]), w=[("betas",)])
        p.dma("sp", lambda e: e.dma_start(out=gws[:], in_=d_gwT[:]), w=[("gws",)])
        p.dma("sp", lambda e: e.dma_start(out=b1s[:], in_=d_b1T[:]), w=[("b1s",)])
        p.dma("sp", lambda e: e.dma_start(out=gts[:], in_=d_gT[:]), w=[("gts",)])
        p.dma("sp", lambda e: e.dma_start(out=bts[:], in_=d_bT[:]), w=[("bts",)])
        p.dma("sp", lambda e: e.dma_start(out=ocb[:], in_=d_ocb[:]), w=[("ocb",)])
        p.dma("sp", lambda e: e.dma_start(out=orb[:], in_=d_orb[:]), w=[("orb",)])
        p.dma("sp", lambda e: e.dma_start(out=orf[:], in_=d_orf[:]), w=[("orf",)])
        p.dma("sp", lambda e: e.dma_start(out=idf[:], in_=d_idf[:]), w=[("idf",)])
        p.dma("sp", lambda e: e.dma_start(out=triSs[:], in_=d_tri[:]), w=[("tri",)])
        p.dma("sp", lambda e: e.dma_start(out=iotas[:], in_=d_iotaI[:]), w=[("iotas",)])
        p.dve(lambda e: e.memset(mgS[0][:], 0.0), w=[("mg", 0)])
        p.dve(lambda e: e.memset(mgS[1][:], 0.0), w=[("mg", 1)])
        p.dma("sp", lambda e: e.dma_start(out=padr[:], in_=d_padrow[:]), w=[("padr",)])
        p.act(lambda e: e.activation(lam[:], betas[:], AF.Exp, scale=NLN2),
              r=[("betas",)], w=[("lam",)])
        p.dve(lambda e: e.tensor_scalar(lam[:], lam[:], -1.0, 1.0, ALU.mult, ALU.add),
              r=[("lam",)], w=[("lam",)])
        p.dve(lambda e: e.tensor_scalar(lam[:], lam[:], TINY, 1.0, ALU.max, ALU.min),
              r=[("lam",)], w=[("lam",)])
        p.dve(lambda e: e.memset(pstate[:], 0.0),
              w=[("pst", cc) for cc in range(NDC)])
        p.dve(lambda e: e.memset(epsc[:], LN_EPS), w=[("epsc",)])

        tr_banks = (4, 5)
        gate_banks = (6, 7)

        def ln_chunk(x_dram, tci, par, boundary, mtv):
            xt = XT[par]
            kx = ("xt", par)
            tc = tci % NTC
            if LN_LIMIT is not None:
                if ln_ctr[0] >= LN_LIMIT:
                    return
                ln_ctr[0] += 1
            sub = 99 if LN_SUB is None else LN_SUB
            negmu = lncols[:, par: par + 1]
            sumsq = lncols[:, 2 + par: 3 + par]
            lnv = lncols[:, 4 + par: 5 + par]
            rstd = lncols[:, 6 + par: 7 + par]
            kc = ("lncols", par)
            p.dma("sp", lambda e: e.dma_start(
                out=xt[:], in_=x_dram[tci * 128:(tci + 1) * 128, :]), w=[kx])
            if sub < 2:
                return
            if sub < 3:
                return
            p.dve(lambda e: e.reduce_sum(negmu[:], xt[:], axis=AX.X, negate=True),
                  r=[kx], w=[kc])
            p.dve(lambda e: e.tensor_scalar(negmu[:], negmu[:], 1.0 / D, None, ALU.mult),
                  r=[kc], w=[kc])
            p.dve(lambda e: e.tensor_scalar(xt[:], xt[:], negmu[:], None, ALU.add),
                  r=[kx, kc], w=[kx])
            p.dve(lambda e: e.scalar_tensor_tensor(
                R2[par][:], xt[:], 1.0, xt[:], ALU.mult, ALU.mult, accum_out=sumsq[:]),
                r=[kx], w=[("r2", par), kc])
            if sub < 4:
                return
            p.act(lambda e: e.activation(lnv[:], sumsq[:], AF.Ln, bias=epsc[:], scale=1.0 / D),
                  r=[kc, ("epsc",)], w=[kc])
            p.act(lambda e: e.activation(rstd[:], lnv[:], AF.Exp, scale=-0.5),
                  r=[kc], w=[kc])
            p.act(lambda e: e.activation(xt[:], xt[:], AF.Identity, scale=rstd[:]),
                  r=[kx, kc], w=[kx])
            if not boundary:
                p.dma("sp", lambda e, tc=tc: e.dma_start(
                    out=d_xnd[tc * 128:(tc + 1) * 128, :], in_=xt[:]),
                    r=[kx], w=[("xnd", tc)])
            if DBG and not boundary:
                p.dma("act", lambda e: e.dma_start(
                    out=d_dbg_xn[(mtv * NTC + tc) * 128:(mtv * NTC + tc + 1) * 128, :],
                    in_=xt[:]), r=[kx])
            if sub < 5:
                return
            for dc in range(NDC):
                tb = tr_banks[dc % 2]
                tpv = pview(tb, 128, 128)
                p.pe(lambda e, dc=dc, tpv=tpv: e.transpose(
                    tpv, xt[:, dc * 128:(dc + 1) * 128], idf[:]),
                    r=[kx, ("idf",)], w=[PBK[tb]])
                if boundary:
                    p.act(lambda e, dc=dc, tpv=tpv, tc=tc: e.activation(
                        xnt[:, dc, tc * 128:(tc + 1) * 128], tpv, AF.Identity),
                        r=[PBK[tb]], w=[("xnt", dc, tc)])
                else:
                    p.act(lambda e, dc=dc, tpv=tpv: e.activation(
                        xf[:, dc, :], tpv, AF.Identity),
                        r=[PBK[tb]], w=[("xf", dc)])
                    p.dve(lambda e, dc=dc, tc=tc: e.tensor_copy(
                        xnt[:, dc, tc * 128:(tc + 1) * 128], xf[:, dc, :]),
                        r=[("xf", dc)], w=[("xnt", dc, tc)])
            if sub < 6 or boundary:
                return
            if True:
                gb = gate_banks[tc % 2]
                gpv = pview(gb, 128, E)
                for dc in range(NDC):
                    p.pe(lambda e, dc=dc, gpv=gpv: e.matmul(
                        gpv, xf[:, dc, :], gws[:, dc, :],
                        start=(dc == 0), stop=(dc == NDC - 1)),
                        r=[("xf", dc), ("gws",)], w=[PBK[gb]])
                gpar = tc % 2
                nmx = gcols[:, gpar: gpar + 1]
                mxc = gcols[:, 2 + gpar: 3 + gpar]
                den = gcols[:, 4 + gpar: 5 + gpar]
                rden = gcols[:, 6 + gpar: 7 + gpar]
                kg = ("gcols", gpar)
                kmw = ("mw", tc)
                p.dve(lambda e, gpv=gpv: e.reduce_max(nmx[:], gpv, axis=AX.X, negate=True),
                      r=[PBK[gb]], w=[kg])
                p.act(lambda e, gpv=gpv, tc=tc: e.activation(
                    mw[:, tc, :], gpv, AF.Exp, bias=nmx[:]),
                    r=[PBK[gb], kg], w=[kmw])
                p.dve(lambda e, tc=tc: e.reduce_sum(den[:], mw[:, tc, :], axis=AX.X),
                      r=[kmw], w=[kg])
                p.dve(lambda e: e.reciprocal(rden[:], den[:]), r=[kg], w=[kg])
                p.dve(lambda e: e.tensor_scalar(mxc[:], nmx[:], -1.0, None, ALU.mult),
                      r=[kg], w=[kg])
                p.dve(lambda e, gpv=gpv, tc=tc: e.tensor_scalar(
                    mw[:, tc, :], gpv, mxc[:], rden[:], ALU.is_ge, ALU.mult),
                    r=[PBK[gb], kg], w=[kmw])
                p.dma("act", lambda e, tc=tc: e.dma_start(
                    out=d_mwd[tc * 128:(tc + 1) * 128, :], in_=mw[:, tc, :]),
                    r=[kmw], w=[("mwd", tc)])
                if DBG:
                    p.dma("act", lambda e, tc=tc: e.dma_start(
                        out=d_dbg_mw[(mtv * NTC + tc) * 128:(mtv * NTC + tc + 1) * 128, :],
                        in_=mw[:, tc, :]), r=[kmw])

        proj_ctr = [0]
        ln_ctr = [0]

        def projection(dsti, wbi, silu=False):
            dst = C[dsti]
            wbuf = C[wbi]
            for ic in range(NDC):
                for half in range(2):
                    bk = proj_ctr[0] % 4
                    proj_ctr[0] += 1
                    ppv = pview(bk, 128, 512)
                    rk0 = [ck(wbi, dc) for dc in range(NDC)] + \
                          [("xnt", dc, t) for dc in range(NDC)
                           for t in range(half * 4, half * 4 + 4)]
                    for dc in range(NDC):
                        p.pe(lambda e, ic=ic, half=half, dc=dc, ppv=ppv, wbuf=wbuf: e.matmul(
                            ppv, wbuf[:, dc, ic * 128:(ic + 1) * 128],
                            xnt[:, dc, half * 512:(half + 1) * 512],
                            start=(dc == 0), stop=(dc == NDC - 1)),
                            r=rk0 if dc == 0 else [ck(wbi, dc)], w=[PBK[bk]])
                    if silu:
                        p.act(lambda e, ic=ic, half=half, ppv=ppv, dst=dst: e.activation(
                            dst[:, ic, half * 512:(half + 1) * 512], ppv, AF.Silu),
                            r=[PBK[bk]], w=[ck(dsti, ic)])
                    else:
                        p.dve(lambda e, ic=ic, half=half, ppv=ppv, dst=dst: e.tensor_copy(
                            dst[:, ic, half * 512:(half + 1) * 512], ppv),
                            r=[PBK[bk]], w=[ck(dsti, ic)])

        def w_dma(dsti, src):
            p.dma("pool", lambda e, dsti=dsti: e.dma_start(
                out=C[dsti][:], in_=src.rearrange("(dc p) i -> p dc i", p=128)),
                w=CALL(dsti))

        def scan_chunk(cc, par, vti):
            ks = ("st", par)
            p.dve(lambda e, cc=cc, par=par, vti=vti: e.tensor_tensor_scan(
                ST[par][:], lam[:, cc:cc + 1].broadcast_to([128, MT]),
                C[vti][:, cc, :], pstate[:, cc:cc + 1], ALU.mult, ALU.add),
                r=[ck(vti, cc), ("lam",), ("pst", cc)], w=[ks])
            p.dve(lambda e, cc=cc, par=par: e.tensor_copy(
                pstate[:, cc:cc + 1], ST[par][:, MT - 1:MT]),
                r=[ks], w=[("pst", cc)])

        if "boundary" in PHASES:
            w_dma(4, d_wT["wvT"])
            for pmt in range(2):
                for tc in range(NTC):
                    ln_chunk(d_xp, pmt * NTC + tc, tc % 2, True, 0)
                projection(1, 4)
                for cc in range(NDC):
                    scan_chunk(cc, cc % 2, 1)

        for mtv in range(2):
            mtbase = mtv * MT
            if "ln" not in PHASES:
                break

            for tc in range(NTC):
                ln_chunk(d_x, mtv * NTC + tc, tc % 2, False, mtv)

            if mtv == 0:
                dbg_probe(0)
            if "qvg" not in PHASES:
                continue
            if mtv == 1 or "boundary" not in PHASES:
                w_dma(4, d_wT["wvT"])
            projection(1, 4)
            w_dma(5, d_wT["wqT"])
            projection(0, 5)
            w_dma(4, d_wT["wgT"])
            projection(2, 4, silu=True)
            w_dma(5, d_wT["woT"])
            if mtv == 0:
                dbg_probe(1)
            if "scan" not in PHASES:
                continue

            for cc in range(NDC):
                par = cc % 2
                scan_chunk(cc, par, 1)
                p.dve(lambda e, cc=cc, par=par: e.tensor_tensor(
                    C[1][:, cc, :], ST[par][:], C[0][:, cc, :], ALU.mult),
                    r=[("st", par), ck(0, cc)], w=[ck(1, cc)])
                p.dve(lambda e, cc=cc, par=par: e.scalar_tensor_tensor(
                    R2[par][:], C[1][:, cc, :], 1.0, C[1][:, cc, :], ALU.mult, ALU.mult),
                    r=[ck(1, cc)], w=[("r2", par)])
                for jt in range(2):
                    mb = 4 + jt
                    sb = 6 + jt
                    p.pe(lambda e, cc=cc, jt=jt, mb=mb: e.matmul(
                        pview(mb, 1, 512), ocb[:], C[1][:, cc, jt * 512:(jt + 1) * 512],
                        start=(cc == 0), stop=(cc == NDC - 1)),
                        r=[ck(1, cc), ("ocb",)], w=[PBK[mb]])
                    p.pe(lambda e, cc=cc, jt=jt, par=par, sb=sb: e.matmul(
                        pview(sb, 1, 512), ocb[:], R2[par][:, jt * 512:(jt + 1) * 512],
                        start=(cc == 0), stop=(cc == NDC - 1)),
                        r=[("r2", par), ("ocb",)], w=[PBK[sb]])

            if mtv == 0:
                dbg_probe(2)
            if "norm" not in PHASES:
                continue
            for jt in range(2):
                mb, sb = 4 + jt, 6 + jt
                ra = rowa[0:1, jt * 512:(jt + 1) * 512]
                rb = rowb[0:1, jt * 512:(jt + 1) * 512]
                p.dve(lambda e, mb=mb, ra=ra: e.tensor_scalar(
                    ra, pview(mb, 1, 512), -1.0 / D, None, ALU.mult),
                    r=[PBK[mb]], w=[("rowa", jt)])
                p.dve(lambda e, ra=ra, rb=rb: e.tensor_tensor(rb, ra, ra, ALU.mult),
                      r=[("rowa", jt)], w=[("rowb", jt)])
                p.dve(lambda e, sb=sb, rb=rb: e.scalar_tensor_tensor(
                    rb, pview(sb, 1, 512), 1.0 / D, rb, ALU.mult, ALU.subtract),
                    r=[PBK[sb], ("rowb", jt)], w=[("rowb", jt)])
                p.act(lambda e, rb=rb: e.activation(rb, rb, AF.Ln, bias=epsc[0:1, 0:1]),
                      r=[("rowb", jt), ("epsc",)], w=[("rowb", jt)])
                p.act(lambda e, rb=rb: e.activation(rb, rb, AF.Exp, scale=-0.5),
                      r=[("rowb", jt)], w=[("rowb", jt)])
                p.pe(lambda e, jt=jt, ra=ra: e.matmul(
                    pview(4 + jt, 128, 512), orf[:], ra, start=True, stop=True),
                    r=[("rowa", jt), ("orf",)], w=[PBK[4 + jt]])
                p.pe(lambda e, jt=jt, rb=rb: e.matmul(
                    pview(6 + jt, 128, 512), orf[:], rb, start=True, stop=True),
                    r=[("rowb", jt), ("orf",)], w=[PBK[6 + jt]])

            for cc in range(NDC):
                par = cc % 2
                sv = ST[par]
                ks = ("st", par)
                p.dve(lambda e, cc=cc, sv=sv: e.tensor_tensor(
                    sv[:], C[1][:, cc, :], pt2[:, :], ALU.add),
                    r=[ck(1, cc), PBK[4], PBK[5]], w=[ks])
                p.dve(lambda e, sv=sv: e.tensor_tensor(sv[:], sv[:], pt3[:, :], ALU.mult),
                      r=[ks, PBK[6], PBK[7]], w=[ks])
                p.dve(lambda e, cc=cc, sv=sv: e.tensor_scalar(
                    sv[:], sv[:], gts[:, cc:cc + 1], bts[:, cc:cc + 1], ALU.mult, ALU.add),
                    r=[ks, ("gts",), ("bts",)], w=[ks])
                p.dve(lambda e, cc=cc, sv=sv: e.tensor_tensor(
                    C[3][:, cc, :], sv[:], C[2][:, cc, :], ALU.mult),
                    r=[ks, ck(2, cc)], w=[ck(3, cc)])

            if mtv == 0:
                dbg_probe(3)
            if "wo" not in PHASES:
                continue
            for tc in range(NTC):
                ybank = (tc % 2) * 2
                ypt = PT[tc % 2]
                ybk = [PBK[ybank], PBK[ybank + 1]]
                for jt in range(2):
                    for ic in range(NDC):
                        p.pe(lambda e, tc=tc, jt=jt, ic=ic, ypt=ypt: e.matmul(
                            ypt[:, jt * 512:(jt + 1) * 512],
                            C[3][:, ic, tc * 128:(tc + 1) * 128],
                            C[5][:, ic, jt * 512:(jt + 1) * 512],
                            start=(ic == 0), stop=(ic == NDC - 1)),
                            r=[ck(3, ic), ck(5, ic)], w=[ybk[jt]])
                par = tc % 2
                kxx = ("xt", par)
                kss = ("st", par)
                p.dma("sp", lambda e, tc=tc, mtbase=mtbase, par=par: e.dma_start(
                    out=XT[par][:], in_=d_x[mtbase + tc * 128: mtbase + (tc + 1) * 128, :]),
                    w=[kxx])
                p.dve(lambda e, tc=tc, ypt=ypt, par=par: e.tensor_tensor(
                    ST[par][:], XT[par][:], ypt[:, :], ALU.add),
                    r=ybk + [kxx], w=[kss])
                p.dma("act", lambda e, tc=tc, mtbase=mtbase, par=par: e.dma_start(
                    out=d_out[mtbase + tc * 128: mtbase + (tc + 1) * 128, :],
                    in_=ST[par][:]), r=[kss], w=[("outd", tc)])
                if DBG:
                    p.dma("act", lambda e, tc=tc, mtbase=mtbase, par=par: e.dma_start(
                        out=d_dbg_attn[(mtbase + tc * 128):(mtbase + (tc + 1) * 128), :],
                        in_=ST[par][:]), r=[kss])

            if DBG and mtv == 0:
                for i in (1, 2, 3, 5):
                    p.dma("pool", lambda e, i=i: e.dma_start(
                        out=d_dbg_c[i][:], in_=C[i][:]),
                        r=[ck(i, s) for s in range(NDC)])
                p.dma("pool", lambda e: e.dma_start(out=d_dbg_rows[0:1, :], in_=rowa[:]),
                      r=[("rowa", 0), ("rowa", 1)])
                p.dma("pool", lambda e: e.dma_start(out=d_dbg_rows[1:2, :], in_=rowb[:]),
                      r=[("rowb", 0), ("rowb", 1)])

            if "ffn" not in PHASES and "route" not in PHASES and "gath" not in PHASES:
                continue

            for ei in range(E):
                kmk = ("mkf",)
                p.dve(lambda e, ei=ei: e.tensor_scalar(
                    mkf[:], mw[:, :, ei], 0.0, None, ALU.is_gt),
                    r=[("mw", t) for t in range(NTC)], w=[kmk])
                p.dve(lambda e: e.tensor_copy(mkb[:], mkf[:]), r=[kmk], w=[("mkb",)])
                p.pe(lambda e: e.matmul(pview(0, 128, NTC), triSs[:], mkb[:],
                                        start=True, stop=True),
                     r=[("mkb",), ("tri",)], w=[PBK[0]])
                p.pe(lambda e: e.matmul(pview(1, 1, NTC), ocb[:], mkb[:],
                                        start=True, stop=True),
                     r=[("mkb",), ("ocb",)], w=[PBK[1]])
                p.dve(lambda e: e.tensor_tensor_scan(
                    crow[0:1, 0:NTC], orf[0:1, 0:NTC], pview(1, 1, NTC), 0.0,
                    ALU.mult, ALU.add), r=[PBK[1], ("orf",)], w=[("crow",)])
                p.dve(lambda e: e.tensor_tensor(
                    crow[0:1, NTC:2 * NTC], crow[0:1, 0:NTC], pview(1, 1, NTC),
                    ALU.subtract), r=[("crow",), PBK[1]], w=[("crow",)])
                p.pe(lambda e: e.matmul(pview(2, 128, NTC), orf[:],
                                        crow[0:1, NTC:2 * NTC], start=True, stop=True),
                     r=[("crow",), ("orf",)], w=[PBK[2]])
                p.dve(lambda e: e.tensor_copy(posw[:], pview(0, 128, NTC)),
                      r=[PBK[0]], w=[("posw",)])
                p.dve(lambda e: e.tensor_tensor(posw[:], posw[:], pview(2, 128, NTC),
                                                ALU.add),
                      r=[("posw",), PBK[2]], w=[("posw",)])
                p.dve(lambda e: e.tensor_scalar(t1w[:], mkf[:], -BIGPOS, BIGPOS,
                                                ALU.mult, ALU.add),
                      r=[kmk], w=[("t1w",)])
                p.dve(lambda e: e.tensor_tensor(posw[:], posw[:], t1w[:], ALU.add),
                      r=[("posw",), ("t1w",)], w=[("posw",)])
                p.dve(lambda e: e.tensor_copy(posi[:], posw[:]),
                      r=[("posw",)], w=[("posi",)])
                p.dma("pool", lambda e, ei=ei: e.dma_start(
                    out=d_gl[ei:ei + 1, :], in_=padr[:]),
                    r=[("padr",)], w=[("gl", ei)])
                for tc in range(NTC):
                    p.dma("pool", lambda e, ei=ei, tc=tc: e.indirect_dma_start(
                        out=d_gl.rearrange("e c -> (e c)").unsqueeze(1),
                        out_offset=IndirectOffsetOnAxis(posi[:, tc:tc + 1], 0),
                        in_=iotas[:, tc:tc + 1], in_offset=None,
                        element_offset=ei * CAP,
                        bounds_check=POOLREGS["cap"], oob_is_err=False),
                        r=[("posi",), ("iotas",)], w=[("gl", ei)])

            def gather_expert(ei, slot):
                kxg = ("xgt", slot)
                for gc in range(NGC):
                    p.dma("pool", lambda e, ei=ei, gc=gc, slot=slot: e.dma_start(
                        out=idxc[slot][:, gc:gc + 1],
                        in_=d_gl[ei, gc * 128:(gc + 1) * 128].unsqueeze(1)),
                        r=[("gl", ei)], w=[("idx", slot, gc)])
                for gc in range(NGC):
                    par = gc % 2
                    kx = ("xt", par)
                    p.dma("pool", lambda e, gc=gc, slot=slot, par=par: e.indirect_dma_start(
                        out=XT[par][:], out_offset=None,
                        in_=d_xnd[:],
                        in_offset=IndirectOffsetOnAxis(idxc[slot][:, gc:gc + 1], 0),
                        bounds_check=POOLREGS["mt"], oob_is_err=False),
                        r=[("idx", slot, gc)] + [("xnd", t) for t in range(NTC)],
                        w=[kx])
                    p.dma("pool", lambda e, gc=gc, slot=slot: e.indirect_dma_start(
                        out=mgS[slot][:, gc, :], out_offset=None,
                        in_=d_mwd[:],
                        in_offset=IndirectOffsetOnAxis(idxc[slot][:, gc:gc + 1], 0),
                        bounds_check=POOLREGS["mt"], oob_is_err=False),
                        r=[("idx", slot, gc)] + [("mwd", t) for t in range(NTC)],
                        w=[("mg", slot)])
                    for dc in range(NDC):
                        tb = 6 + dc % 2
                        p.pe(lambda e, gc=gc, dc=dc, par=par, tb=tb: e.transpose(
                            pview(tb, 128, 128), XT[par][:, dc * 128:(dc + 1) * 128],
                            idf[:]),
                            r=[kx, ("idf",)], w=[PBK[tb]])
                        p.act(lambda e, gc=gc, dc=dc, slot=slot, tb=tb: e.activation(
                            xgt[slot][:, dc, gc * 128:(gc + 1) * 128],
                            pview(tb, 128, 128), AF.Identity),
                            r=[PBK[tb]], w=[kxg])

            def h1v(fc32, cols=None):
                ch = C[fc32 // 16]
                sl = (fc32 % 16) // 2
                off = (fc32 % 2) * CAP
                if cols is None:
                    return ch[:, sl, off:off + CAP]
                return ch[:, sl, off + cols[0]:off + cols[1]]

            def h1key(fc32):
                return ck(fc32 // 16, (fc32 % 16) // 2)

            stages = [(ei, h) for ei in range(E) for h in range(2)]
            w18ctr = [0]
            stage_w1 = {}

            def stage_w2_dmas(si):
                ei, h = stages[si]
                if h == 0:
                    b2par = ei % 2
                    p.dma("pool", lambda e, ei=ei, b2par=b2par: e.dma_start(
                        out=B2S[b2par][:], in_=d_b2[ei:ei + 1, :]), w=[("b2s", b2par)])
                w2pair = (2, 3) if si % 2 == 0 else (4, 5)
                for ci in range(2):
                    cdst = C[w2pair[ci]]
                    p.dma("pool", lambda e, ei=ei, h=h, ci=ci, cdst=cdst: e.dma_start(
                        out=cdst[:],
                        in_=d_w2[ei][h * 2048 + ci * 1024: h * 2048 + (ci + 1) * 1024, :]
                        .rearrange("(fc p) c -> p fc c", p=128)),
                        w=CALL(w2pair[ci]))

            def stage_w1_dmas(si, ks):
                ei, h = stages[si]
                lst = stage_w1.setdefault(si, {})
                for k8 in ks:
                    wpar = w18ctr[0] % 2
                    w18ctr[0] += 1
                    wb = W1B[wpar]
                    wbk = ("w1b", wpar)
                    f0 = h * 2048 + k8 * 512
                    p.dma("pool", lambda e, ei=ei, f0=f0, wb=wb: e.dma_start(
                        out=wb[:],
                        in_=d_w1[ei][:, f0:f0 + 512].rearrange("(dc p) f -> p dc f", p=128)),
                        w=[wbk])
                    lst[k8] = (wb, wbk)

            if "ffn" not in PHASES and "gath" not in PHASES:
                continue
            gather_expert(0, 0)
            if "ffn" not in PHASES:
                continue
            stage_w2_dmas(0)
            stage_w1_dmas(0, (0, 1))
            nstages = len(stages) if FFN_STAGES is None else min(FFN_STAGES, len(stages))
            for si in range(nstages):
                ei, h = stages[si]
                b2par = ei % 2
                slot = ei % 2
                w2pair = (2, 3) if si % 2 == 0 else (4, 5)
                ygb = [(0, PT[0]), (2, PT[1]), (4, PT[2])]
                for k8 in range(4):
                    if k8 == 2:
                        stage_w1_dmas(si, (2, 3))
                    wb, wbk = stage_w1[si][k8]
                    for fcl in range(4):
                        fc32 = h * 16 + k8 * 4 + fcl
                        hb = 6 + fcl % 2
                        hpv = pview(hb, 128, CAP)
                        for dc in range(NDC):
                            p.pe(lambda e, fcl=fcl, dc=dc, wb=wb, hpv=hpv,
                                 slot=slot: e.matmul(
                                     hpv, wb[:, dc, fcl * 128:(fcl + 1) * 128],
                                     xgt[slot][:, dc, :],
                                     start=(dc == 0), stop=(dc == NDC - 1)),
                                r=[wbk, ("xgt", slot)], w=[PBK[hb]])
                        p.act(lambda e, fc32=fc32, hpv=hpv, ei=ei: e.activation(
                            h1v(fc32), hpv, AF.Relu, bias=b1s[:, ei, fc32:fc32 + 1]),
                            r=[PBK[hb], ("b1s",)], w=[h1key(fc32)])
                if h == 0 and ei + 1 < E and (FFN_STAGES is None or si + 2 < nstages):
                    gather_expert(ei + 1, (ei + 1) % 2)
                if si + 1 < nstages:
                    stage_w1_dmas(si + 1, (0, 1))
                    stage_w2_dmas(si + 1)
                for gc in range(NGC):
                    b0, ypt = ygb[gc]
                    ybk = [PBK[b0], PBK[b0 + 1]]
                    if h == 0:
                        for jt in range(2):
                            p.pe(lambda e, jt=jt, ypt=ypt, b2par=b2par: e.matmul(
                                ypt[:, jt * 512:(jt + 1) * 512], orb[:],
                                B2S[b2par][0:1, jt * 512:(jt + 1) * 512],
                                start=True, stop=False),
                                r=[("b2s", b2par), ("orb",)], w=[ybk[jt]])
                    for fch in range(16):
                        fc32 = h * 16 + fch
                        w2i = w2pair[fch // 8]
                        for jt in range(2):
                            stp = (h == 1 and fch == 15)
                            p.pe(lambda e, fc32=fc32, gc=gc, jt=jt, ypt=ypt,
                                 w2i=w2i, fch=fch, stp=stp: e.matmul(
                                     ypt[:, jt * 512:(jt + 1) * 512],
                                     h1v(fc32, (gc * 128, (gc + 1) * 128)),
                                     C[w2i][:, fch % 8, jt * 512:(jt + 1) * 512],
                                     start=False,
                                     stop=stp),
                                r=[h1key(fc32), ck(w2i, fch % 8)],
                                w=[ybk[jt]])
                if h == 1:
                    for gc in range(NGC):
                        b0, ypt = ygb[gc]
                        ybk = [PBK[b0], PBK[b0 + 1]]
                        yt = ST[gc % 2]
                        kyt = ("st", gc % 2)
                        kyb = ("ytb", gc % 2)
                        p.act(lambda e, gc=gc, ypt=ypt, slot=slot, ei=ei, yt=yt: e.activation(
                            yt[:], ypt[:, :], AF.Identity,
                            scale=mgS[slot][:, gc, ei:ei + 1]),
                            r=ybk + [("mg", slot)], w=[kyt])
                        if NO_SCATTER:
                            continue
                        p.dma("pool", lambda e, gc=gc, slot=slot,
                              mtbase=mtbase: e.indirect_dma_start(
                                  out=ytb[gc % 2][:], out_offset=None,
                                  in_=d_out[0:mtbase + MT, :],
                                  in_offset=IndirectOffsetOnAxis(
                                      idxc[slot][:, gc:gc + 1], 0),
                                  element_offset=mtbase * D,
                                  bounds_check=POOLREGS["mt"], oob_is_err=False),
                              r=[("idx", slot, gc)]
                              + [("outd", t) for t in range(NTC)],
                              w=[kyb])
                        p.dve(lambda e, gc=gc, yt=yt: e.tensor_tensor(
                            yt[:], yt[:], ytb[gc % 2][:], ALU.add),
                            r=[kyt, kyb], w=[kyt])
                        p.dma("pool", lambda e, gc=gc, slot=slot, yt=yt,
                              mtbase=mtbase: e.indirect_dma_start(
                                  out=d_out[:],
                                  out_offset=IndirectOffsetOnAxis(
                                      idxc[slot][:, gc:gc + 1], 0),
                                  in_=yt[:], in_offset=None,
                                  element_offset=mtbase * D,
                                  bounds_check=POOLREGS["mt"], oob_is_err=False),
                              r=[kyt, ("idx", slot, gc), ("outsc",)],
                              w=[("outsc",)])

        p.finalize(block, sems)
    return nc


def kernel(L0_attn_x, L0_attn_Wq, L0_attn_Wv, L0_attn_Wo, L0_attn_Wg,
           L0_attn_beta, L0_attn_ln_gamma, L0_attn_ln_beta,
           L0_ffn_gate_w, L0_ffn_w1, L0_ffn_b1, L0_ffn_w2, L0_ffn_b2):
    f32 = np.float32
    x = np.asarray(L0_attn_x, dtype=f32)
    common = {
        "wqT": np.ascontiguousarray(np.asarray(L0_attn_Wq, f32).T),
        "wvT": np.ascontiguousarray(np.asarray(L0_attn_Wv, f32).T),
        "wgT": np.ascontiguousarray(np.asarray(L0_attn_Wg, f32).T),
        "woT": np.ascontiguousarray(np.asarray(L0_attn_Wo, f32).T),
        "gwT": np.ascontiguousarray(
            np.asarray(L0_ffn_gate_w, f32).T.reshape(NDC, 128, E).transpose(1, 0, 2)),
        "betaT": np.ascontiguousarray(
            np.repeat(np.asarray(L0_attn_beta, f32), Dh).reshape(NDC, 128).T),
        "gT": np.ascontiguousarray(np.asarray(L0_attn_ln_gamma, f32).reshape(NDC, 128).T),
        "bT": np.ascontiguousarray(np.asarray(L0_attn_ln_beta, f32).reshape(NDC, 128).T),
        "w1": np.ascontiguousarray(np.asarray(L0_ffn_w1, f32)),
        "b1T": np.ascontiguousarray(
            np.asarray(L0_ffn_b1, f32).reshape(E, DFF // 128, 128).transpose(2, 0, 1)),
        "w2": np.ascontiguousarray(np.asarray(L0_ffn_w2, f32)),
        "b2": np.ascontiguousarray(np.asarray(L0_ffn_b2, f32)),
        "ones_col_bf": np.ones((128, 1), dtype=ml_dtypes.bfloat16),
        "ones_row_bf": np.ones((1, 128), dtype=ml_dtypes.bfloat16),
        "ones_row_f": np.ones((1, 128), dtype=f32),
        "iden_f": np.eye(128, dtype=f32),
        "triS": np.triu(np.ones((128, 128)), k=1).astype(ml_dtypes.bfloat16),
        "iotaI": np.ascontiguousarray(
            (np.arange(NTC)[None, :] * 128 + np.arange(128)[:, None]).astype(np.int32)),
        "padrow": np.full((1, CAP), PADIDX, dtype=np.int32),
    }
    zeros = np.zeros((TPC, D), dtype=f32)
    in_maps = []
    for core in range(8):
        b, half = core // 2, core % 2
        m = dict(common)
        m["x"] = np.ascontiguousarray(x[b, half * TPC:(half + 1) * TPC, :])
        m["xp"] = zeros if half == 0 else np.ascontiguousarray(x[b, 0:TPC, :])
        in_maps.append(m)

    nc = build()
    res = run_bass_kernel_spmd(nc, in_maps, core_ids=list(range(8)))
    out = np.empty((B, L, D), dtype=f32)
    for core in range(8):
        b, half = core // 2, core % 2
        out[b, half * TPC:(half + 1) * TPC, :] = res.results[core]["out"]
    if DBG:
        kernel.dbg = res.results
    return out


# version 08
# speedup vs baseline: 1.4858x; 1.1290x over previous
import numpy as np
import ml_dtypes
import concourse.bass as bass
import concourse.mybir as mybir
from concourse.bass_utils import run_bass_kernel_spmd
from concourse.bass import IndirectOffsetOnAxis

F32 = mybir.dt.float32
BF16 = mybir.dt.bfloat16
I32 = mybir.dt.int32
AF = mybir.ActivationFunctionType
ALU = mybir.AluOpType
AX = mybir.AxisListType

B, L, D, H, Dh = 4, 4096, 1024, 16, 64
E, DFF = 8, 4096
TPC = 2048
MT = 1024
NTC = MT // 128
NDC = D // 128
LN_EPS = 1e-5
TINY = 1.1754943508222875e-38
NLN2 = -0.6931471805599453

DBG = False
PHASES = frozenset(("boundary", "ln", "qvg", "scan", "norm", "wo", "ffn"))
CAP = 384
NGC = CAP // 128
PADIDX = 20000
BIGPOS = 100000.0
LN_LIMIT = None
NO_SCATTER = False
SCATTER_OP = "add"
FFN_STAGES = None
LN_SUB = None

ENGINES = ("pe", "act", "dve", "pool", "sp")
ENG_ATTR = {"pe": "tensor", "act": "scalar", "dve": "vector", "pool": "gpsimd", "sp": "sync"}
DMA_Q = {"pool": "qw", "sp": "qx", "act": "qo"}
NQSEM = 6


class _Op:
    __slots__ = ("stream", "emit", "deps", "q", "inc", "val", "waits", "post",
                 "seq", "token")


class Prog:

    def __init__(self):
        self.pool_prologue = None
        self.ops = []
        self.lw = {}
        self.lr = {}
        self.qcount = {"qw": 0, "qx": 0, "qo": 0}
        self.quse = {}
        self.qlast = {}

    def _add(self, stream, emit, r, w, q=None):
        op = _Op()
        op.stream, op.emit, op.q = stream, emit, q
        op.seq = len(self.ops)
        deps = set()
        for k in r:
            d = self.lw.get(k)
            if d is not None:
                deps.add(d)
        for k in w:
            d = self.lw.get(k)
            if d is not None:
                deps.add(d)
            for x in self.lr.get(k, ()):
                deps.add(x)
        for k in r:
            self.lr.setdefault(k, []).append(op)
        for k in w:
            self.lw[k] = op
            self.lr[k] = []
        if q is not None:
            qi = self.qcount[q] % NQSEM
            self.qcount[q] += 1
            sem_id = (q, qi)
            use = self.quse.get(sem_id, 0) + 1
            self.quse[sem_id] = use
            op.val = 16 * use
            op.token = (sem_id, op.val)
            prev = self.qlast.get(sem_id)
            if prev is not None:
                deps.add(prev)
            self.qlast[sem_id] = op
        deps.discard(op)
        op.deps = deps
        self.ops.append(op)
        return op

    def pe(self, emit, r=(), w=()):
        return self._add("pe", emit, r, w)

    def act(self, emit, r=(), w=()):
        return self._add("act", emit, r, w)

    def dve(self, emit, r=(), w=()):
        return self._add("dve", emit, r, w)

    def pool(self, emit, r=(), w=()):
        return self._add("pool", emit, r, w)

    def sp(self, emit, r=(), w=()):
        return self._add("sp", emit, r, w)

    def dma(self, eng, emit, r=(), w=()):
        return self._add(eng, emit, r, w, q=DMA_Q[eng])

    def finalize(self, block, sems):
        for op in self.ops:
            op.inc = op.q is not None
        for op in self.ops:
            for d in op.deps:
                if d.q is None and not (d.stream == "pe" and op.stream == "pe"):
                    d.inc = True
        counters = {k: 0 for k in ENGINES}
        for op in self.ops:
            if op.q is not None:
                pass
            elif op.inc:
                counters[op.stream] += 1
                op.val = counters[op.stream]
                op.token = (op.stream, op.val)
            else:
                op.val, op.token = None, None
        finals = dict(self.quse)
        clocks = {e: {} for e in ENGINES}
        for op in self.ops:
            clk = clocks[op.stream]
            waits = []
            for d in sorted(op.deps, key=lambda o: o.seq):
                if d.q is None and d.stream == "pe" and op.stream == "pe":
                    continue
                sem, val = d.token
                if clk.get(sem, 0) >= val:
                    continue
                waits.append((sem, val))
                for s2, v2 in d.post.items():
                    if clk.get(s2, 0) < v2:
                        clk[s2] = v2
                if clk.get(sem, 0) < val:
                    clk[sem] = val
            op.waits = waits
            post = dict(clk)
            if op.token is not None:
                s, v = op.token
                if post.get(s, 0) < v:
                    post[s] = v
            op.post = post

        def mksec(stream):
            def sec(eng):
                if stream == "pool" and self.pool_prologue is not None:
                    self.pool_prologue(eng)
                for op in self.ops:
                    if op.stream != stream:
                        continue
                    for sem_name, val in op.waits:
                        eng.wait_ge(sems[sem_name], val)
                    try:
                        inst = op.emit(eng)
                    except Exception as ex:
                        raise RuntimeError(f"emit failed seq={op.seq} stream={op.stream}: {ex}") from ex
                    if op.q is not None:
                        inst.then_inc(sems[op.token[0]], 16)
                    elif op.inc:
                        inst.then_inc(sems[op.stream], 1)
                for sem_id, uses in finals.items():
                    eng.wait_ge(sems[sem_id], 16 * uses)
            return sec

        for stream in ENGINES:
            getattr(block, ENG_ATTR[stream])(mksec(stream))


def build():
    nc = bass.Bass("TRN2", target_bir_lowering=False)

    d_x = nc.dram_tensor("x", [TPC, D], F32, kind="ExternalInput").ap()
    d_xp = nc.dram_tensor("xp", [TPC, D], F32, kind="ExternalInput").ap()
    d_wT = {}
    for nm in ("wqT", "wvT", "wgT", "woT"):
        d_wT[nm] = nc.dram_tensor(nm, [D, D], F32, kind="ExternalInput").ap()
    d_gwT = nc.dram_tensor("gwT", [128, NDC, E], F32, kind="ExternalInput").ap()
    d_betaT = nc.dram_tensor("betaT", [128, NDC], F32, kind="ExternalInput").ap()
    d_gT = nc.dram_tensor("gT", [128, NDC], F32, kind="ExternalInput").ap()
    d_bT = nc.dram_tensor("bT", [128, NDC], F32, kind="ExternalInput").ap()
    d_w1 = nc.dram_tensor("w1", [E, D, DFF], F32, kind="ExternalInput").ap()
    d_b1T = nc.dram_tensor("b1T", [128, E, DFF // 128], F32, kind="ExternalInput").ap()
    d_w2 = nc.dram_tensor("w2", [E, DFF, D], F32, kind="ExternalInput").ap()
    d_b2 = nc.dram_tensor("b2", [E, D], F32, kind="ExternalInput").ap()
    d_ocb = nc.dram_tensor("ones_col_bf", [128, 1], BF16, kind="ExternalInput").ap()
    d_orb = nc.dram_tensor("ones_row_bf", [1, 128], BF16, kind="ExternalInput").ap()
    d_orf = nc.dram_tensor("ones_row_f", [1, 128], F32, kind="ExternalInput").ap()
    d_idf = nc.dram_tensor("iden_f", [128, 128], F32, kind="ExternalInput").ap()
    d_tri = nc.dram_tensor("triS", [128, 128], BF16, kind="ExternalInput").ap()
    d_iotaI = nc.dram_tensor("iotaI", [128, NTC], I32, kind="ExternalInput").ap()
    d_padrow = nc.dram_tensor("padrow", [1, CAP], I32, kind="ExternalInput").ap()
    d_xnd = nc.dram_tensor("xnd", [TPC, D], F32, kind="Internal").ap()
    d_mwd = nc.dram_tensor("mwd", [TPC, E], F32, kind="Internal").ap()
    d_gl = nc.dram_tensor("glist", [2 * E, CAP], I32, kind="Internal").ap()
    d_out = nc.dram_tensor("out", [TPC, D], F32, kind="ExternalOutput").ap()
    if DBG:
        d_dbg_xn = nc.dram_tensor("dbg_xn", [TPC, D], F32, kind="ExternalOutput").ap()
        d_dbg_mw = nc.dram_tensor("dbg_mw", [TPC, E], F32, kind="ExternalOutput").ap()
        d_dbg_attn = nc.dram_tensor("dbg_attn", [TPC, D], F32, kind="ExternalOutput").ap()
        d_dbg_c = {i: nc.dram_tensor(f"dbg_c{i}", [128, NDC, MT], F32,
                                     kind="ExternalOutput").ap() for i in (1, 2, 3, 5)}
        d_dbg_rows = nc.dram_tensor("dbg_rows", [3, MT], F32, kind="ExternalOutput").ap()
        d_dbg_acc0 = nc.dram_tensor("dbg_acc0", [128, D], F32, kind="ExternalOutput").ap()
        d_dbg_apsum = nc.dram_tensor("dbg_apsum", [128, D], F32, kind="ExternalOutput").ap()
        d_dbg_probe = nc.dram_tensor("dbg_probe", [6, 128, D], F32, kind="ExternalOutput").ap()

    from contextlib import ExitStack
    with ExitStack() as _stk:
        sb = lambda *a: _stk.enter_context(nc.sbuf_tensor(*a))
        ps = lambda *a: _stk.enter_context(nc.psum_tensor(*a))
        sm = lambda n: _stk.enter_context(nc.semaphore(n))
        c0 = sb("c0", [128, NDC, MT], BF16)
        c1 = sb("c1", [128, NDC, MT], BF16)
        c2 = sb("c2", [128, NDC, MT], BF16)
        c3 = sb("c3", [128, NDC, MT], BF16)
        c4 = sb("c4", [128, NDC, MT], BF16)
        c5 = sb("c5", [128, NDC, MT], BF16)
        w1b0 = sb("w1b0", [128, NDC, 512], BF16)
        w1b1 = sb("w1b1", [128, NDC, 512], BF16)
        xnu = sb("xnu", [128, 4 * NDC * CAP], BF16)
        xnt = xnu[:, 0:NDC * MT].rearrange("p (dc t) -> p dc t", t=MT)

        def XGT(slot):
            return xnu[:, slot * NDC * CAP:(slot + 1) * NDC * CAP].rearrange(
                "p (dc c) -> p dc c", c=CAP)
        xt0 = sb("xt0", [128, D], F32)
        xt1 = sb("xt1", [128, D], F32)
        st0 = sb("st0", [128, MT], F32)
        st1 = sb("st1", [128, MT], F32)
        r20 = sb("r20", [128, MT], BF16)
        r21 = sb("r21", [128, MT], BF16)
        xf = sb("xf", [128, 8, 128], F32)
        rowa = sb("rowa", [1, MT], F32)
        rowb = sb("rowb", [1, MT], F32)

        lncols = sb("lncols", [128, 8], F32)
        gcols = sb("gcols", [128, 8], F32)
        mw = sb("mw", [128, NTC, E], F32)
        lam = sb("lam", [128, NDC], F32)
        betas = sb("betas", [128, NDC], F32)
        gws = sb("gws", [128, NDC, E], F32)
        b1s = sb("b1s", [128, E, DFF // 128], F32)
        gts = sb("gts", [128, NDC], F32)
        bts = sb("bts", [128, NDC], F32)
        pstate = sb("pstate", [128, NDC], F32)
        b2s0 = sb("b2s0", [1, D], BF16)
        b2s1 = sb("b2s1", [1, D], BF16)
        ocb = sb("ocb", [128, 1], BF16)
        orb = sb("orb", [1, 128], BF16)
        orf = sb("orf", [1, 128], F32)
        idf = sb("idf", [128, 128], F32)
        epsc = sb("epsc", [128, 1], F32)
        idxc = [sb(f"idxc{i}", [128, NGC], I32) for i in range(4)]
        mgS = [sb(f"mg{i}", [128, NGC, E], F32) for i in range(4)]
        ysb = sb("ysb", [128, 2, NGC, D], BF16)
        mkf = sb("mkf", [128, NTC], F32)
        mkb = sb("mkb", [128, NTC], BF16)
        posw = sb("posw", [128, NTC], F32)
        t1w = sb("t1w", [128, NTC], F32)
        posi = sb("posi", [128, NTC], I32)
        crow = sb("crow", [1, 2 * NTC], F32)
        triSs = sb("triSs", [128, 128], BF16)
        padr = sb("padr", [1, CAP], I32)
        ytb = [sb("ytb0", [128, D], F32), sb("ytb1", [128, D], F32)]
        iotas = sb("iotas", [128, NTC], I32)
        pt0 = ps("pt0", [128, 1024], F32)
        pt1 = ps("pt1", [128, 1024], F32)
        pt2 = ps("pt2", [128, 1024], F32)
        pt3 = ps("pt3", [128, 1024], F32)
        sems = {}
        for e in ENGINES:
            sems[e] = sm(f"s_{e}")
        for qn in ("qw", "qx", "qo"):
            for i in range(NQSEM):
                sems[(qn, i)] = sm(f"s_{qn}{i}")
        block = _stk.enter_context(nc.Block())
        C = [c0, c1, c2, c3, c4, c5]
        W1B = [w1b0, w1b1]
        XT = [xt0, xt1]
        ST = [st0, st1]
        R2 = [r20, r21]
        B2S = [b2s0, b2s1]
        PT = [pt0, pt1, pt2, pt3]
        p = Prog()
        POOLREGS = {}

        def _pl(eng):
            POOLREGS["cap"] = eng.to_reg(CAP - 1)
            POOLREGS["mt"] = eng.to_reg(MT - 1)
        p.pool_prologue = _pl

        def pview(b, rows, cols, coloff=0):
            t, off = PT[b // 2], (b % 2) * 512
            return t[0:rows, off + coloff: off + coloff + cols]

        PBK = [("pb", i) for i in range(8)]

        def ck(ci, slot):
            return ("c", ci, slot)

        CALL = lambda ci: [ck(ci, s) for s in range(NDC)]

        def dbg_probe(k):
            pass

        p.dma("sp", lambda e: e.dma_start(out=betas[:], in_=d_betaT[:]), w=[("betas",)])
        p.dma("sp", lambda e: e.dma_start(out=gws[:], in_=d_gwT[:]), w=[("gws",)])
        p.dma("sp", lambda e: e.dma_start(out=b1s[:], in_=d_b1T[:]), w=[("b1s",)])
        p.dma("sp", lambda e: e.dma_start(out=gts[:], in_=d_gT[:]), w=[("gts",)])
        p.dma("sp", lambda e: e.dma_start(out=bts[:], in_=d_bT[:]), w=[("bts",)])
        p.dma("sp", lambda e: e.dma_start(out=ocb[:], in_=d_ocb[:]), w=[("ocb",)])
        p.dma("sp", lambda e: e.dma_start(out=orb[:], in_=d_orb[:]), w=[("orb",)])
        p.dma("sp", lambda e: e.dma_start(out=orf[:], in_=d_orf[:]), w=[("orf",)])
        p.dma("sp", lambda e: e.dma_start(out=idf[:], in_=d_idf[:]), w=[("idf",)])
        p.dma("sp", lambda e: e.dma_start(out=triSs[:], in_=d_tri[:]), w=[("tri",)])
        p.dma("sp", lambda e: e.dma_start(out=iotas[:], in_=d_iotaI[:]), w=[("iotas",)])
        for _i in range(4):
            p.dve(lambda e, _i=_i: e.memset(mgS[_i][:], 0.0), w=[("mg", _i)])
        p.dma("sp", lambda e: e.dma_start(out=padr[:], in_=d_padrow[:]), w=[("padr",)])
        p.act(lambda e: e.activation(lam[:], betas[:], AF.Exp, scale=NLN2),
              r=[("betas",)], w=[("lam",)])
        p.dve(lambda e: e.tensor_scalar(lam[:], lam[:], -1.0, 1.0, ALU.mult, ALU.add),
              r=[("lam",)], w=[("lam",)])
        p.dve(lambda e: e.tensor_scalar(lam[:], lam[:], TINY, 1.0, ALU.max, ALU.min),
              r=[("lam",)], w=[("lam",)])
        p.dve(lambda e: e.memset(pstate[:], 0.0),
              w=[("pst", cc) for cc in range(NDC)])
        p.dve(lambda e: e.memset(epsc[:], LN_EPS), w=[("epsc",)])

        tr_banks = (4, 5)
        gate_banks = (6, 7)

        def ln_chunk(x_dram, tci, par, boundary, mtv):
            xt = XT[par]
            kx = ("xt", par)
            tc = tci % NTC
            if LN_LIMIT is not None:
                if ln_ctr[0] >= LN_LIMIT:
                    return
                ln_ctr[0] += 1
            sub = 99 if LN_SUB is None else LN_SUB
            negmu = lncols[:, par: par + 1]
            sumsq = lncols[:, 2 + par: 3 + par]
            lnv = lncols[:, 4 + par: 5 + par]
            rstd = lncols[:, 6 + par: 7 + par]
            kc = ("lncols", par)
            p.dma("sp", lambda e: e.dma_start(
                out=xt[:], in_=x_dram[tci * 128:(tci + 1) * 128, :]), w=[kx])
            if sub < 2:
                return
            if sub < 3:
                return
            p.dve(lambda e: e.reduce_sum(negmu[:], xt[:], axis=AX.X, negate=True),
                  r=[kx], w=[kc])
            p.dve(lambda e: e.tensor_scalar(negmu[:], negmu[:], 1.0 / D, None, ALU.mult),
                  r=[kc], w=[kc])
            p.dve(lambda e: e.tensor_scalar(xt[:], xt[:], negmu[:], None, ALU.add),
                  r=[kx, kc], w=[kx])
            p.dve(lambda e: e.scalar_tensor_tensor(
                R2[par][:], xt[:], 1.0, xt[:], ALU.mult, ALU.mult, accum_out=sumsq[:]),
                r=[kx], w=[("r2", par), kc])
            if sub < 4:
                return
            p.act(lambda e: e.activation(lnv[:], sumsq[:], AF.Ln, bias=epsc[:], scale=1.0 / D),
                  r=[kc, ("epsc",)], w=[kc])
            p.act(lambda e: e.activation(rstd[:], lnv[:], AF.Exp, scale=-0.5),
                  r=[kc], w=[kc])
            p.act(lambda e: e.activation(xt[:], xt[:], AF.Identity, scale=rstd[:]),
                  r=[kx, kc], w=[kx])
            if not boundary:
                p.dma("sp", lambda e, tc=tc, mtv=mtv: e.dma_start(
                    out=d_xnd[(mtv * NTC + tc) * 128:(mtv * NTC + tc + 1) * 128, :],
                    in_=xt[:]),
                    r=[kx], w=[("xnd", mtv, tc)])
            if DBG and not boundary:
                p.dma("act", lambda e: e.dma_start(
                    out=d_dbg_xn[(mtv * NTC + tc) * 128:(mtv * NTC + tc + 1) * 128, :],
                    in_=xt[:]), r=[kx])
            if sub < 5:
                return
            for dc in range(NDC):
                tb = tr_banks[dc % 2]
                tpv = pview(tb, 128, 128)
                p.pe(lambda e, dc=dc, tpv=tpv: e.transpose(
                    tpv, xt[:, dc * 128:(dc + 1) * 128], idf[:]),
                    r=[kx, ("idf",)], w=[PBK[tb]])
                if boundary:
                    p.act(lambda e, dc=dc, tpv=tpv, tc=tc: e.activation(
                        xnt[:, dc, tc * 128:(tc + 1) * 128], tpv, AF.Identity),
                        r=[PBK[tb]], w=[("xnt", dc, tc)])
                else:
                    p.act(lambda e, dc=dc, tpv=tpv: e.activation(
                        xf[:, dc, :], tpv, AF.Identity),
                        r=[PBK[tb]], w=[("xf", dc)])
                    p.dve(lambda e, dc=dc, tc=tc: e.tensor_copy(
                        xnt[:, dc, tc * 128:(tc + 1) * 128], xf[:, dc, :]),
                        r=[("xf", dc)], w=[("xnt", dc, tc)])
            if sub < 6 or boundary:
                return
            if True:
                gb = gate_banks[tc % 2]
                gpv = pview(gb, 128, E)
                for dc in range(NDC):
                    p.pe(lambda e, dc=dc, gpv=gpv: e.matmul(
                        gpv, xf[:, dc, :], gws[:, dc, :],
                        start=(dc == 0), stop=(dc == NDC - 1)),
                        r=[("xf", dc), ("gws",)], w=[PBK[gb]])
                gpar = tc % 2
                nmx = gcols[:, gpar: gpar + 1]
                mxc = gcols[:, 2 + gpar: 3 + gpar]
                den = gcols[:, 4 + gpar: 5 + gpar]
                rden = gcols[:, 6 + gpar: 7 + gpar]
                kg = ("gcols", gpar)
                kmw = ("mw", tc)
                p.dve(lambda e, gpv=gpv: e.reduce_max(nmx[:], gpv, axis=AX.X, negate=True),
                      r=[PBK[gb]], w=[kg])
                p.act(lambda e, gpv=gpv, tc=tc: e.activation(
                    mw[:, tc, :], gpv, AF.Exp, bias=nmx[:]),
                    r=[PBK[gb], kg], w=[kmw])
                p.dve(lambda e, tc=tc: e.reduce_sum(den[:], mw[:, tc, :], axis=AX.X),
                      r=[kmw], w=[kg])
                p.dve(lambda e: e.reciprocal(rden[:], den[:]), r=[kg], w=[kg])
                p.dve(lambda e: e.tensor_scalar(mxc[:], nmx[:], -1.0, None, ALU.mult),
                      r=[kg], w=[kg])
                p.dve(lambda e, gpv=gpv, tc=tc: e.tensor_scalar(
                    mw[:, tc, :], gpv, mxc[:], rden[:], ALU.is_ge, ALU.mult),
                    r=[PBK[gb], kg], w=[kmw])
                p.dma("act", lambda e, tc=tc, mtv=mtv: e.dma_start(
                    out=d_mwd[(mtv * NTC + tc) * 128:(mtv * NTC + tc + 1) * 128, :],
                    in_=mw[:, tc, :]),
                    r=[kmw], w=[("mwd", mtv, tc)])
                if DBG:
                    p.dma("act", lambda e, tc=tc: e.dma_start(
                        out=d_dbg_mw[(mtv * NTC + tc) * 128:(mtv * NTC + tc + 1) * 128, :],
                        in_=mw[:, tc, :]), r=[kmw])

        proj_ctr = [0]
        ln_ctr = [0]

        def projection(dsti, wbi, silu=False):
            dst = C[dsti]
            wbuf = C[wbi]
            for ic in range(NDC):
                for half in range(2):
                    bk = proj_ctr[0] % 4
                    proj_ctr[0] += 1
                    ppv = pview(bk, 128, 512)
                    rk0 = [ck(wbi, dc) for dc in range(NDC)] + \
                          [("xnt", dc, t) for dc in range(NDC)
                           for t in range(half * 4, half * 4 + 4)]
                    for dc in range(NDC):
                        p.pe(lambda e, ic=ic, half=half, dc=dc, ppv=ppv, wbuf=wbuf: e.matmul(
                            ppv, wbuf[:, dc, ic * 128:(ic + 1) * 128],
                            xnt[:, dc, half * 512:(half + 1) * 512],
                            start=(dc == 0), stop=(dc == NDC - 1)),
                            r=rk0 if dc == 0 else [ck(wbi, dc)], w=[PBK[bk]])
                    if silu:
                        p.act(lambda e, ic=ic, half=half, ppv=ppv, dst=dst: e.activation(
                            dst[:, ic, half * 512:(half + 1) * 512], ppv, AF.Silu),
                            r=[PBK[bk]], w=[ck(dsti, ic)])
                    else:
                        p.dve(lambda e, ic=ic, half=half, ppv=ppv, dst=dst: e.tensor_copy(
                            dst[:, ic, half * 512:(half + 1) * 512], ppv),
                            r=[PBK[bk]], w=[ck(dsti, ic)])

        def w_dma(dsti, src):
            p.dma("pool", lambda e, dsti=dsti: e.dma_start(
                out=C[dsti][:], in_=src.rearrange("(dc p) i -> p dc i", p=128)),
                w=CALL(dsti))

        def scan_chunk(cc, par, vti):
            ks = ("st", par)
            p.dve(lambda e, cc=cc, par=par, vti=vti: e.tensor_tensor_scan(
                ST[par][:], lam[:, cc:cc + 1].broadcast_to([128, MT]),
                C[vti][:, cc, :], pstate[:, cc:cc + 1], ALU.mult, ALU.add),
                r=[ck(vti, cc), ("lam",), ("pst", cc)], w=[ks])
            p.dve(lambda e, cc=cc, par=par: e.tensor_copy(
                pstate[:, cc:cc + 1], ST[par][:, MT - 1:MT]),
                r=[ks], w=[("pst", cc)])

        if "boundary" in PHASES:
            w_dma(4, d_wT["wvT"])
            for pmt in range(2):
                for tc in range(NTC):
                    ln_chunk(d_xp, pmt * NTC + tc, tc % 2, True, 0)
                projection(1, 4)
                for cc in range(NDC):
                    scan_chunk(cc, cc % 2, 1)

        for mtv in range(2):
            mtbase = mtv * MT
            if "ln" not in PHASES:
                break

            for tc in range(NTC):
                ln_chunk(d_x, mtv * NTC + tc, tc % 2, False, mtv)

            if mtv == 0:
                dbg_probe(0)
            if "qvg" not in PHASES:
                continue
            if mtv == 1 or "boundary" not in PHASES:
                w_dma(4, d_wT["wvT"])
            projection(1, 4)
            w_dma(5, d_wT["wqT"])
            projection(0, 5)
            w_dma(4, d_wT["wgT"])
            projection(2, 4, silu=True)
            w_dma(5, d_wT["woT"])
            if mtv == 0:
                dbg_probe(1)
            if "scan" not in PHASES:
                continue

            for cc in range(NDC):
                par = cc % 2
                scan_chunk(cc, par, 1)
                p.dve(lambda e, cc=cc, par=par: e.tensor_tensor(
                    C[1][:, cc, :], ST[par][:], C[0][:, cc, :], ALU.mult),
                    r=[("st", par), ck(0, cc)], w=[ck(1, cc)])
                p.dve(lambda e, cc=cc, par=par: e.scalar_tensor_tensor(
                    R2[par][:], C[1][:, cc, :], 1.0, C[1][:, cc, :], ALU.mult, ALU.mult),
                    r=[ck(1, cc)], w=[("r2", par)])
                for jt in range(2):
                    mb = 4 + jt
                    sb = 6 + jt
                    p.pe(lambda e, cc=cc, jt=jt, mb=mb: e.matmul(
                        pview(mb, 1, 512), ocb[:], C[1][:, cc, jt * 512:(jt + 1) * 512],
                        start=(cc == 0), stop=(cc == NDC - 1)),
                        r=[ck(1, cc), ("ocb",)], w=[PBK[mb]])
                    p.pe(lambda e, cc=cc, jt=jt, par=par, sb=sb: e.matmul(
                        pview(sb, 1, 512), ocb[:], R2[par][:, jt * 512:(jt + 1) * 512],
                        start=(cc == 0), stop=(cc == NDC - 1)),
                        r=[("r2", par), ("ocb",)], w=[PBK[sb]])

            if mtv == 0:
                dbg_probe(2)
            if "norm" not in PHASES:
                continue
            for jt in range(2):
                mb, sb = 4 + jt, 6 + jt
                ra = rowa[0:1, jt * 512:(jt + 1) * 512]
                rb = rowb[0:1, jt * 512:(jt + 1) * 512]
                p.dve(lambda e, mb=mb, ra=ra: e.tensor_scalar(
                    ra, pview(mb, 1, 512), -1.0 / D, None, ALU.mult),
                    r=[PBK[mb]], w=[("rowa", jt)])
                p.dve(lambda e, ra=ra, rb=rb: e.tensor_tensor(rb, ra, ra, ALU.mult),
                      r=[("rowa", jt)], w=[("rowb", jt)])
                p.dve(lambda e, sb=sb, rb=rb: e.scalar_tensor_tensor(
                    rb, pview(sb, 1, 512), 1.0 / D, rb, ALU.mult, ALU.subtract),
                    r=[PBK[sb], ("rowb", jt)], w=[("rowb", jt)])
                p.act(lambda e, rb=rb: e.activation(rb, rb, AF.Ln, bias=epsc[0:1, 0:1]),
                      r=[("rowb", jt), ("epsc",)], w=[("rowb", jt)])
                p.act(lambda e, rb=rb: e.activation(rb, rb, AF.Exp, scale=-0.5),
                      r=[("rowb", jt)], w=[("rowb", jt)])
                p.pe(lambda e, jt=jt, ra=ra: e.matmul(
                    pview(4 + jt, 128, 512), orf[:], ra, start=True, stop=True),
                    r=[("rowa", jt), ("orf",)], w=[PBK[4 + jt]])
                p.pe(lambda e, jt=jt, rb=rb: e.matmul(
                    pview(6 + jt, 128, 512), orf[:], rb, start=True, stop=True),
                    r=[("rowb", jt), ("orf",)], w=[PBK[6 + jt]])

            for cc in range(NDC):
                par = cc % 2
                sv = ST[par]
                ks = ("st", par)
                p.dve(lambda e, cc=cc, sv=sv: e.tensor_tensor(
                    sv[:], C[1][:, cc, :], pt2[:, :], ALU.add),
                    r=[ck(1, cc), PBK[4], PBK[5]], w=[ks])
                p.dve(lambda e, sv=sv: e.tensor_tensor(sv[:], sv[:], pt3[:, :], ALU.mult),
                      r=[ks, PBK[6], PBK[7]], w=[ks])
                p.dve(lambda e, cc=cc, sv=sv: e.tensor_scalar(
                    sv[:], sv[:], gts[:, cc:cc + 1], bts[:, cc:cc + 1], ALU.mult, ALU.add),
                    r=[ks, ("gts",), ("bts",)], w=[ks])
                p.dve(lambda e, cc=cc, sv=sv: e.tensor_tensor(
                    C[3][:, cc, :], sv[:], C[2][:, cc, :], ALU.mult),
                    r=[ks, ck(2, cc)], w=[ck(3, cc)])

            if mtv == 0:
                dbg_probe(3)
            if "wo" not in PHASES:
                continue
            for tc in range(NTC):
                ybank = (tc % 2) * 2
                ypt = PT[tc % 2]
                ybk = [PBK[ybank], PBK[ybank + 1]]
                for jt in range(2):
                    for ic in range(NDC):
                        p.pe(lambda e, tc=tc, jt=jt, ic=ic, ypt=ypt: e.matmul(
                            ypt[:, jt * 512:(jt + 1) * 512],
                            C[3][:, ic, tc * 128:(tc + 1) * 128],
                            C[5][:, ic, jt * 512:(jt + 1) * 512],
                            start=(ic == 0), stop=(ic == NDC - 1)),
                            r=[ck(3, ic), ck(5, ic)], w=[ybk[jt]])
                par = tc % 2
                kxx = ("xt", par)
                kss = ("st", par)
                p.dma("sp", lambda e, tc=tc, mtbase=mtbase, par=par: e.dma_start(
                    out=XT[par][:], in_=d_x[mtbase + tc * 128: mtbase + (tc + 1) * 128, :]),
                    w=[kxx])
                p.dve(lambda e, tc=tc, ypt=ypt, par=par: e.tensor_tensor(
                    ST[par][:], XT[par][:], ypt[:, :], ALU.add),
                    r=ybk + [kxx], w=[kss])
                p.dma("act", lambda e, tc=tc, mtbase=mtbase, par=par: e.dma_start(
                    out=d_out[mtbase + tc * 128: mtbase + (tc + 1) * 128, :],
                    in_=ST[par][:]), r=[kss], w=[("outd", mtv, tc)])
                if DBG:
                    p.dma("act", lambda e, tc=tc, mtbase=mtbase, par=par: e.dma_start(
                        out=d_dbg_attn[(mtbase + tc * 128):(mtbase + (tc + 1) * 128), :],
                        in_=ST[par][:]), r=[kss])

            if DBG and mtv == 0:
                for i in (1, 2, 3, 5):
                    p.dma("pool", lambda e, i=i: e.dma_start(
                        out=d_dbg_c[i][:], in_=C[i][:]),
                        r=[ck(i, s) for s in range(NDC)])
                p.dma("pool", lambda e: e.dma_start(out=d_dbg_rows[0:1, :], in_=rowa[:]),
                      r=[("rowa", 0), ("rowa", 1)])
                p.dma("pool", lambda e: e.dma_start(out=d_dbg_rows[1:2, :], in_=rowb[:]),
                      r=[("rowb", 0), ("rowb", 1)])

            if "ffn" not in PHASES and "route" not in PHASES and "gath" not in PHASES:
                continue
            for ei in range(E):
                kmk = ("mkf",)
                p.dve(lambda e, ei=ei: e.tensor_scalar(
                    mkf[:], mw[:, :, ei], 0.0, None, ALU.is_gt),
                    r=[("mw", t) for t in range(NTC)], w=[kmk])
                p.dve(lambda e: e.tensor_copy(mkb[:], mkf[:]), r=[kmk], w=[("mkb",)])
                p.pe(lambda e: e.matmul(pview(0, 128, NTC), triSs[:], mkb[:],
                                        start=True, stop=True),
                     r=[("mkb",), ("tri",)], w=[PBK[0]])
                p.pe(lambda e: e.matmul(pview(1, 1, NTC), ocb[:], mkb[:],
                                        start=True, stop=True),
                     r=[("mkb",), ("ocb",)], w=[PBK[1]])
                p.dve(lambda e: e.tensor_tensor_scan(
                    crow[0:1, 0:NTC], orf[0:1, 0:NTC], pview(1, 1, NTC), 0.0,
                    ALU.mult, ALU.add), r=[PBK[1], ("orf",)], w=[("crow",)])
                p.dve(lambda e: e.tensor_tensor(
                    crow[0:1, NTC:2 * NTC], crow[0:1, 0:NTC], pview(1, 1, NTC),
                    ALU.subtract), r=[("crow",), PBK[1]], w=[("crow",)])
                p.pe(lambda e: e.matmul(pview(2, 128, NTC), orf[:],
                                        crow[0:1, NTC:2 * NTC], start=True, stop=True),
                     r=[("crow",), ("orf",)], w=[PBK[2]])
                p.dve(lambda e: e.tensor_copy(posw[:], pview(0, 128, NTC)),
                      r=[PBK[0]], w=[("posw",)])
                p.dve(lambda e: e.tensor_tensor(posw[:], posw[:], pview(2, 128, NTC),
                                                ALU.add),
                      r=[("posw",), PBK[2]], w=[("posw",)])
                p.dve(lambda e: e.tensor_scalar(t1w[:], mkf[:], -BIGPOS, BIGPOS,
                                                ALU.mult, ALU.add),
                      r=[kmk], w=[("t1w",)])
                p.dve(lambda e: e.tensor_tensor(posw[:], posw[:], t1w[:], ALU.add),
                      r=[("posw",), ("t1w",)], w=[("posw",)])
                p.dve(lambda e: e.tensor_copy(posi[:], posw[:]),
                      r=[("posw",)], w=[("posi",)])
                gr = mtv * E + ei
                p.dma("pool", lambda e, gr=gr: e.dma_start(
                    out=d_gl[gr:gr + 1, :], in_=padr[:]),
                    r=[("padr",)], w=[("gl", gr)])
                for tc in range(NTC):
                    p.dma("pool", lambda e, gr=gr, tc=tc: e.indirect_dma_start(
                        out=d_gl.rearrange("e c -> (e c)").unsqueeze(1),
                        out_offset=IndirectOffsetOnAxis(posi[:, tc:tc + 1], 0),
                        in_=iotas[:, tc:tc + 1], in_offset=None,
                        element_offset=gr * CAP,
                        bounds_check=POOLREGS["cap"], oob_is_err=False),
                        r=[("posi",), ("iotas",)], w=[("gl", gr)])

        if "ffn" in PHASES or "gath" in PHASES:
            XNTKEYS = [("xnt", dc, t) for dc in range(NDC) for t in range(NTC)]

            def gather_expert(ei, mtv):
                slot = 2 * (ei % 2) + mtv
                mtbase = mtv * MT
                gr = mtv * E + ei
                kxg = ("xgt", slot)
                for gc in range(NGC):
                    p.dma("pool", lambda e, gr=gr, gc=gc, slot=slot: e.dma_start(
                        out=idxc[slot][:, gc:gc + 1],
                        in_=d_gl[gr, gc * 128:(gc + 1) * 128].unsqueeze(1)),
                        r=[("gl", gr)], w=[("idx", slot, gc)])
                for gc in range(NGC):
                    par = gc % 2
                    kx = ("xt", par)
                    p.dma("pool", lambda e, gc=gc, slot=slot, par=par,
                          mtbase=mtbase: e.indirect_dma_start(
                        out=XT[par][:], out_offset=None,
                        in_=d_xnd[:],
                        in_offset=IndirectOffsetOnAxis(idxc[slot][:, gc:gc + 1], 0),
                        element_offset=mtbase * D,
                        bounds_check=POOLREGS["mt"], oob_is_err=False),
                        r=[("idx", slot, gc)]
                        + [("xnd", m, t) for m in range(2) for t in range(NTC)],
                        w=[kx])
                    p.dma("pool", lambda e, gc=gc, slot=slot,
                          mtbase=mtbase: e.indirect_dma_start(
                        out=mgS[slot][:, gc, :], out_offset=None,
                        in_=d_mwd[:],
                        in_offset=IndirectOffsetOnAxis(idxc[slot][:, gc:gc + 1], 0),
                        element_offset=mtbase * E,
                        bounds_check=POOLREGS["mt"], oob_is_err=False),
                        r=[("idx", slot, gc)]
                        + [("mwd", m, t) for m in range(2) for t in range(NTC)],
                        w=[("mg", slot)])
                    for dc in range(NDC):
                        tb = 6 + dc % 2
                        p.pe(lambda e, gc=gc, dc=dc, par=par, tb=tb: e.transpose(
                            pview(tb, 128, 128), XT[par][:, dc * 128:(dc + 1) * 128],
                            idf[:]),
                            r=[("xt", par), ("idf",)], w=[PBK[tb]])
                        p.act(lambda e, gc=gc, dc=dc, slot=slot, tb=tb: e.activation(
                            XGT(slot)[:, dc, gc * 128:(gc + 1) * 128],
                            pview(tb, 128, 128), AF.Identity),
                            r=[PBK[tb]], w=[kxg] + (XNTKEYS if True else []))

            def h1v(mtv, fc16, cols=None):
                ch = C[mtv]
                sl = fc16 // 2
                off = (fc16 % 2) * CAP
                if cols is None:
                    return ch[:, sl, off:off + CAP]
                return ch[:, sl, off + cols[0]:off + cols[1]]

            def h1key(mtv, fc16):
                return ck(mtv, fc16 // 2)

            stages = [(ei, h) for ei in range(E) for h in range(2)]
            w18ctr = [0]
            stage_w1 = {}

            def stage_w2_dmas(si):
                ei, h = stages[si]
                if h == 0:
                    b2par = ei % 2
                    p.dma("pool", lambda e, ei=ei, b2par=b2par: e.dma_start(
                        out=B2S[b2par][:], in_=d_b2[ei:ei + 1, :]), w=[("b2s", b2par)])
                w2pair = (2, 3) if si % 2 == 0 else (4, 5)
                for ci in range(2):
                    cdst = C[w2pair[ci]]
                    p.dma("pool", lambda e, ei=ei, h=h, ci=ci, cdst=cdst: e.dma_start(
                        out=cdst[:],
                        in_=d_w2[ei][h * 2048 + ci * 1024: h * 2048 + (ci + 1) * 1024, :]
                        .rearrange("(fc p) c -> p fc c", p=128)),
                        w=CALL(w2pair[ci]))

            def stage_w1_dmas(si, ks):
                ei, h = stages[si]
                lst = stage_w1.setdefault(si, {})
                for k8 in ks:
                    wpar = w18ctr[0] % 2
                    w18ctr[0] += 1
                    wb = W1B[wpar]
                    wbk = ("w1b", wpar)
                    f0 = h * 2048 + k8 * 512
                    p.dma("pool", lambda e, ei=ei, f0=f0, wb=wb: e.dma_start(
                        out=wb[:],
                        in_=d_w1[ei][:, f0:f0 + 512].rearrange("(dc p) f -> p dc f", p=128)),
                        w=[wbk])
                    lst[k8] = (wb, wbk)

            gather_expert(0, 0)
            gather_expert(0, 1)
            if "ffn" in PHASES:
                stage_w2_dmas(0)
                stage_w1_dmas(0, (0, 1))
                nstages = len(stages) if FFN_STAGES is None else min(FFN_STAGES, len(stages))
                for si in range(nstages):
                    ei, h = stages[si]
                    b2par = ei % 2
                    w2pair = (2, 3) if si % 2 == 0 else (4, 5)
                    for k8 in range(4):
                        if k8 == 2:
                            stage_w1_dmas(si, (2, 3))
                        wb, wbk = stage_w1[si][k8]
                        for fcl in range(4):
                            fc16 = k8 * 4 + fcl
                            fc32 = h * 16 + fc16
                            for mtv in range(2):
                                slot = 2 * (ei % 2) + mtv
                                hb = 6 + (fcl * 2 + mtv) % 2
                                hpv = pview(hb, 128, CAP)
                                for dc in range(NDC):
                                    p.pe(lambda e, fcl=fcl, dc=dc, wb=wb, hpv=hpv,
                                         slot=slot: e.matmul(
                                             hpv, wb[:, dc, fcl * 128:(fcl + 1) * 128],
                                             XGT(slot)[:, dc, :],
                                             start=(dc == 0), stop=(dc == NDC - 1)),
                                        r=[wbk, ("xgt", slot)], w=[PBK[hb]])
                                p.act(lambda e, mtv=mtv, fc16=fc16, fc32=fc32,
                                      hpv=hpv, ei=ei: e.activation(
                                          h1v(mtv, fc16), hpv, AF.Relu,
                                          bias=b1s[:, ei, fc32:fc32 + 1]),
                                      r=[PBK[hb], ("b1s",)], w=[h1key(mtv, fc16)])
                    if h == 0 and ei + 1 < E and (FFN_STAGES is None or si + 2 < nstages):
                        gather_expert(ei + 1, 0)
                        gather_expert(ei + 1, 1)
                    if si + 1 < nstages:
                        stage_w1_dmas(si + 1, (0, 1))
                        stage_w2_dmas(si + 1)
                    for mtv in range(2):
                        for gc in range(NGC):
                            gi = mtv * NGC + gc
                            ypt = PT[gi % 2]
                            ybk = [PBK[(gi % 2) * 2], PBK[(gi % 2) * 2 + 1]]
                            for jt in range(2):
                                p.pe(lambda e, jt=jt, ypt=ypt, b2par=b2par: e.matmul(
                                    ypt[:, jt * 512:(jt + 1) * 512], orb[:],
                                    B2S[b2par][0:1, jt * 512:(jt + 1) * 512],
                                    start=True, stop=False),
                                    r=[("b2s", b2par), ("orb",)], w=[ybk[jt]])
                            for fc16 in range(16):
                                w2i = w2pair[fc16 // 8]
                                for jt in range(2):
                                    p.pe(lambda e, mtv=mtv, fc16=fc16, gc=gc, jt=jt,
                                         ypt=ypt, w2i=w2i: e.matmul(
                                             ypt[:, jt * 512:(jt + 1) * 512],
                                             h1v(mtv, fc16, (gc * 128, (gc + 1) * 128)),
                                             C[w2i][:, fc16 % 8, jt * 512:(jt + 1) * 512],
                                             start=False, stop=(fc16 == 15)),
                                        r=[h1key(mtv, fc16), ck(w2i, fc16 % 8)],
                                        w=[ybk[jt]])
                            ky = ("ysb", mtv, gc)
                            if h == 0:
                                p.dve(lambda e, mtv=mtv, gc=gc, ypt=ypt: e.tensor_copy(
                                    ysb[:, mtv, gc, :], ypt[:, :]),
                                    r=ybk, w=[ky])
                            else:
                                p.dve(lambda e, mtv=mtv, gc=gc, ypt=ypt:
                                      e.scalar_tensor_tensor(
                                          ysb[:, mtv, gc, :], ypt[:, :], 1.0,
                                          ysb[:, mtv, gc, :], ALU.mult, ALU.add),
                                    r=ybk + [ky], w=[ky])
                    if h == 1:
                        for mtv in range(2):
                            slot = 2 * (ei % 2) + mtv
                            mtbase = mtv * MT
                            for gc in range(NGC):
                                gi = mtv * NGC + gc
                                ky = ("ysb", mtv, gc)
                                yt = ST[gi % 2]
                                kyt = ("st", gi % 2)
                                kyb = ("ytb", gi % 2)
                                p.act(lambda e, mtv=mtv, gc=gc, slot=slot, ei=ei,
                                      yt=yt: e.activation(
                                          yt[:], ysb[:, mtv, gc, :], AF.Identity,
                                          scale=mgS[slot][:, gc, ei:ei + 1]),
                                      r=[ky, ("mg", slot)], w=[kyt])
                                p.dma("pool", lambda e, gc=gc, slot=slot,
                                      mtbase=mtbase, gi=gi: e.indirect_dma_start(
                                          out=ytb[gi % 2][:], out_offset=None,
                                          in_=d_out[:],
                                          in_offset=IndirectOffsetOnAxis(
                                              idxc[slot][:, gc:gc + 1], 0),
                                          element_offset=mtbase * D,
                                          bounds_check=POOLREGS["mt"], oob_is_err=False),
                                      r=[("idx", slot, gc)]
                                      + [("outd", m, t) for m in range(2)
                                         for t in range(NTC)],
                                      w=[kyb])
                                p.dve(lambda e, gi=gi, yt=yt: e.tensor_tensor(
                                    yt[:], yt[:], ytb[gi % 2][:], ALU.add),
                                    r=[kyt, kyb], w=[kyt])
                                p.dma("pool", lambda e, gc=gc, slot=slot, yt=yt,
                                      mtbase=mtbase, gi=gi: e.indirect_dma_start(
                                          out=d_out[:],
                                          out_offset=IndirectOffsetOnAxis(
                                              idxc[slot][:, gc:gc + 1], 0),
                                          in_=yt[:], in_offset=None,
                                          element_offset=mtbase * D,
                                          bounds_check=POOLREGS["mt"], oob_is_err=False),
                                      r=[kyt, ("idx", slot, gc), ("outsc",)],
                                      w=[("outsc",)])

        p.finalize(block, sems)
    return nc


def kernel(L0_attn_x, L0_attn_Wq, L0_attn_Wv, L0_attn_Wo, L0_attn_Wg,
           L0_attn_beta, L0_attn_ln_gamma, L0_attn_ln_beta,
           L0_ffn_gate_w, L0_ffn_w1, L0_ffn_b1, L0_ffn_w2, L0_ffn_b2):
    f32 = np.float32
    x = np.asarray(L0_attn_x, dtype=f32)
    common = {
        "wqT": np.ascontiguousarray(np.asarray(L0_attn_Wq, f32).T),
        "wvT": np.ascontiguousarray(np.asarray(L0_attn_Wv, f32).T),
        "wgT": np.ascontiguousarray(np.asarray(L0_attn_Wg, f32).T),
        "woT": np.ascontiguousarray(np.asarray(L0_attn_Wo, f32).T),
        "gwT": np.ascontiguousarray(
            np.asarray(L0_ffn_gate_w, f32).T.reshape(NDC, 128, E).transpose(1, 0, 2)),
        "betaT": np.ascontiguousarray(
            np.repeat(np.asarray(L0_attn_beta, f32), Dh).reshape(NDC, 128).T),
        "gT": np.ascontiguousarray(np.asarray(L0_attn_ln_gamma, f32).reshape(NDC, 128).T),
        "bT": np.ascontiguousarray(np.asarray(L0_attn_ln_beta, f32).reshape(NDC, 128).T),
        "w1": np.ascontiguousarray(np.asarray(L0_ffn_w1, f32)),
        "b1T": np.ascontiguousarray(
            np.asarray(L0_ffn_b1, f32).reshape(E, DFF // 128, 128).transpose(2, 0, 1)),
        "w2": np.ascontiguousarray(np.asarray(L0_ffn_w2, f32)),
        "b2": np.ascontiguousarray(np.asarray(L0_ffn_b2, f32)),
        "ones_col_bf": np.ones((128, 1), dtype=ml_dtypes.bfloat16),
        "ones_row_bf": np.ones((1, 128), dtype=ml_dtypes.bfloat16),
        "ones_row_f": np.ones((1, 128), dtype=f32),
        "iden_f": np.eye(128, dtype=f32),
        "triS": np.triu(np.ones((128, 128)), k=1).astype(ml_dtypes.bfloat16),
        "iotaI": np.ascontiguousarray(
            (np.arange(NTC)[None, :] * 128 + np.arange(128)[:, None]).astype(np.int32)),
        "padrow": np.full((1, CAP), PADIDX, dtype=np.int32),
    }
    zeros = np.zeros((TPC, D), dtype=f32)
    in_maps = []
    for core in range(8):
        b, half = core // 2, core % 2
        m = dict(common)
        m["x"] = np.ascontiguousarray(x[b, half * TPC:(half + 1) * TPC, :])
        m["xp"] = zeros if half == 0 else np.ascontiguousarray(x[b, 0:TPC, :])
        in_maps.append(m)

    nc = build()
    res = run_bass_kernel_spmd(nc, in_maps, core_ids=list(range(8)))
    out = np.empty((B, L, D), dtype=f32)
    for core in range(8):
        b, half = core // 2, core % 2
        out[b, half * TPC:(half + 1) * TPC, :] = res.results[core]["out"]
    if DBG:
        kernel.dbg = res.results
    return out


# version 09
# speedup vs baseline: 1.4893x; 1.0023x over previous
import numpy as np
import ml_dtypes
import concourse.bass as bass
import concourse.mybir as mybir
from concourse.bass_utils import run_bass_kernel_spmd
from concourse.bass import IndirectOffsetOnAxis

F32 = mybir.dt.float32
BF16 = mybir.dt.bfloat16
I32 = mybir.dt.int32
AF = mybir.ActivationFunctionType
ALU = mybir.AluOpType
AX = mybir.AxisListType

B, L, D, H, Dh = 4, 4096, 1024, 16, 64
E, DFF = 8, 4096
TPC = 2048
MT = 1024
NTC = MT // 128
NDC = D // 128
LN_EPS = 1e-5
TINY = 1.1754943508222875e-38
NLN2 = -0.6931471805599453

DBG = False
PHASES = frozenset(("boundary", "ln", "qvg", "scan", "norm", "wo", "ffn"))
CAP = 384
NGC = CAP // 128
PADIDX = 20000
BIGPOS = 100000.0
LN_LIMIT = None
NO_SCATTER = False
SCATTER_OP = "add"
FFN_STAGES = None
LN_SUB = None

ENGINES = ("pe", "act", "dve", "pool", "sp")
ENG_ATTR = {"pe": "tensor", "act": "scalar", "dve": "vector", "pool": "gpsimd", "sp": "sync"}
DMA_Q = {"pool": "qw", "sp": "qx", "act": "qo"}
NQSEM = 6


class _Op:
    __slots__ = ("stream", "emit", "deps", "q", "inc", "val", "waits", "post",
                 "seq", "token")


class Prog:

    def __init__(self):
        self.pool_prologue = None
        self.ops = []
        self.lw = {}
        self.lr = {}
        self.qcount = {"qw": 0, "qx": 0, "qo": 0}
        self.quse = {}
        self.qlast = {}

    def _add(self, stream, emit, r, w, q=None):
        op = _Op()
        op.stream, op.emit, op.q = stream, emit, q
        op.seq = len(self.ops)
        deps = set()
        for k in r:
            d = self.lw.get(k)
            if d is not None:
                deps.add(d)
        for k in w:
            d = self.lw.get(k)
            if d is not None:
                deps.add(d)
            for x in self.lr.get(k, ()):
                deps.add(x)
        for k in r:
            self.lr.setdefault(k, []).append(op)
        for k in w:
            self.lw[k] = op
            self.lr[k] = []
        if q is not None:
            qi = self.qcount[q] % NQSEM
            self.qcount[q] += 1
            sem_id = (q, qi)
            use = self.quse.get(sem_id, 0) + 1
            self.quse[sem_id] = use
            op.val = 16 * use
            op.token = (sem_id, op.val)
            prev = self.qlast.get(sem_id)
            if prev is not None:
                deps.add(prev)
            self.qlast[sem_id] = op
        deps.discard(op)
        op.deps = deps
        self.ops.append(op)
        return op

    def pe(self, emit, r=(), w=()):
        return self._add("pe", emit, r, w)

    def act(self, emit, r=(), w=()):
        return self._add("act", emit, r, w)

    def dve(self, emit, r=(), w=()):
        return self._add("dve", emit, r, w)

    def pool(self, emit, r=(), w=()):
        return self._add("pool", emit, r, w)

    def sp(self, emit, r=(), w=()):
        return self._add("sp", emit, r, w)

    def dma(self, eng, emit, r=(), w=()):
        return self._add(eng, emit, r, w, q=DMA_Q[eng])

    def finalize(self, block, sems):
        for op in self.ops:
            op.inc = op.q is not None
        for op in self.ops:
            for d in op.deps:
                if d.q is None and not (d.stream == "pe" and op.stream == "pe"):
                    d.inc = True
        counters = {k: 0 for k in ENGINES}
        for op in self.ops:
            if op.q is not None:
                pass
            elif op.inc:
                counters[op.stream] += 1
                op.val = counters[op.stream]
                op.token = (op.stream, op.val)
            else:
                op.val, op.token = None, None
        finals = dict(self.quse)
        clocks = {e: {} for e in ENGINES}
        for op in self.ops:
            clk = clocks[op.stream]
            waits = []
            for d in sorted(op.deps, key=lambda o: o.seq):
                if d.q is None and d.stream == "pe" and op.stream == "pe":
                    continue
                sem, val = d.token
                if clk.get(sem, 0) >= val:
                    continue
                waits.append((sem, val))
                for s2, v2 in d.post.items():
                    if clk.get(s2, 0) < v2:
                        clk[s2] = v2
                if clk.get(sem, 0) < val:
                    clk[sem] = val
            op.waits = waits
            post = dict(clk)
            if op.token is not None:
                s, v = op.token
                if post.get(s, 0) < v:
                    post[s] = v
            op.post = post

        def mksec(stream):
            def sec(eng):
                if stream == "pool" and self.pool_prologue is not None:
                    self.pool_prologue(eng)
                for op in self.ops:
                    if op.stream != stream:
                        continue
                    for sem_name, val in op.waits:
                        eng.wait_ge(sems[sem_name], val)
                    try:
                        inst = op.emit(eng)
                    except Exception as ex:
                        raise RuntimeError(f"emit failed seq={op.seq} stream={op.stream}: {ex}") from ex
                    if op.q is not None:
                        inst.then_inc(sems[op.token[0]], 16)
                    elif op.inc:
                        inst.then_inc(sems[op.stream], 1)
                for sem_id, uses in finals.items():
                    eng.wait_ge(sems[sem_id], 16 * uses)
            return sec

        for stream in ENGINES:
            getattr(block, ENG_ATTR[stream])(mksec(stream))


def build():
    nc = bass.Bass("TRN2", target_bir_lowering=False)

    d_x = nc.dram_tensor("x", [TPC, D], F32, kind="ExternalInput").ap()
    d_xp = nc.dram_tensor("xp", [TPC, D], F32, kind="ExternalInput").ap()
    d_wT = {}
    for nm in ("wqT", "wvT", "wgT", "woT"):
        d_wT[nm] = nc.dram_tensor(nm, [D, D], F32, kind="ExternalInput").ap()
    d_gwT = nc.dram_tensor("gwT", [128, NDC, E], F32, kind="ExternalInput").ap()
    d_betaT = nc.dram_tensor("betaT", [128, NDC], F32, kind="ExternalInput").ap()
    d_gT = nc.dram_tensor("gT", [128, NDC], F32, kind="ExternalInput").ap()
    d_bT = nc.dram_tensor("bT", [128, NDC], F32, kind="ExternalInput").ap()
    d_w1 = nc.dram_tensor("w1", [E, D, DFF], F32, kind="ExternalInput").ap()
    d_b1T = nc.dram_tensor("b1T", [128, E, DFF // 128], F32, kind="ExternalInput").ap()
    d_w2 = nc.dram_tensor("w2", [E, DFF, D], F32, kind="ExternalInput").ap()
    d_b2 = nc.dram_tensor("b2", [E, D], F32, kind="ExternalInput").ap()
    d_ocb = nc.dram_tensor("ones_col_bf", [128, 1], BF16, kind="ExternalInput").ap()
    d_orb = nc.dram_tensor("ones_row_bf", [1, 128], BF16, kind="ExternalInput").ap()
    d_orf = nc.dram_tensor("ones_row_f", [1, 128], F32, kind="ExternalInput").ap()
    d_idf = nc.dram_tensor("iden_f", [128, 128], F32, kind="ExternalInput").ap()
    d_tri = nc.dram_tensor("triS", [128, 128], BF16, kind="ExternalInput").ap()
    d_iotaI = nc.dram_tensor("iotaI", [128, NTC], I32, kind="ExternalInput").ap()
    d_padrow = nc.dram_tensor("padrow", [1, CAP], I32, kind="ExternalInput").ap()
    d_xnd = nc.dram_tensor("xnd", [TPC, D], F32, kind="Internal").ap()
    d_mwd = nc.dram_tensor("mwd", [TPC, E], F32, kind="Internal").ap()
    d_gl = nc.dram_tensor("glist", [2 * E, CAP], I32, kind="Internal").ap()
    d_out = nc.dram_tensor("out", [TPC, D], F32, kind="ExternalOutput").ap()
    if DBG:
        d_dbg_xn = nc.dram_tensor("dbg_xn", [TPC, D], F32, kind="ExternalOutput").ap()
        d_dbg_mw = nc.dram_tensor("dbg_mw", [TPC, E], F32, kind="ExternalOutput").ap()
        d_dbg_attn = nc.dram_tensor("dbg_attn", [TPC, D], F32, kind="ExternalOutput").ap()
        d_dbg_c = {i: nc.dram_tensor(f"dbg_c{i}", [128, NDC, MT], F32,
                                     kind="ExternalOutput").ap() for i in (1, 2, 3, 5)}
        d_dbg_rows = nc.dram_tensor("dbg_rows", [3, MT], F32, kind="ExternalOutput").ap()
        d_dbg_acc0 = nc.dram_tensor("dbg_acc0", [128, D], F32, kind="ExternalOutput").ap()
        d_dbg_apsum = nc.dram_tensor("dbg_apsum", [128, D], F32, kind="ExternalOutput").ap()
        d_dbg_probe = nc.dram_tensor("dbg_probe", [6, 128, D], F32, kind="ExternalOutput").ap()

    from contextlib import ExitStack
    with ExitStack() as _stk:
        sb = lambda *a: _stk.enter_context(nc.sbuf_tensor(*a))
        ps = lambda *a: _stk.enter_context(nc.psum_tensor(*a))
        sm = lambda n: _stk.enter_context(nc.semaphore(n))
        c0 = sb("c0", [128, NDC, MT], BF16)
        c1 = sb("c1", [128, NDC, MT], BF16)
        c2 = sb("c2", [128, NDC, MT], BF16)
        c3 = sb("c3", [128, NDC, MT], BF16)
        c4 = sb("c4", [128, NDC, MT], BF16)
        c5 = sb("c5", [128, NDC, MT], BF16)
        w1b0 = sb("w1b0", [128, NDC, 512], BF16)
        w1b1 = sb("w1b1", [128, NDC, 512], BF16)
        xnu = sb("xnu", [128, 4 * NDC * CAP], BF16)
        xnt = xnu[:, 0:NDC * MT].rearrange("p (dc t) -> p dc t", t=MT)

        def XGT(slot):
            return xnu[:, slot * NDC * CAP:(slot + 1) * NDC * CAP].rearrange(
                "p (dc c) -> p dc c", c=CAP)
        xt0 = sb("xt0", [128, D], F32)
        xt1 = sb("xt1", [128, D], F32)
        st0 = sb("st0", [128, MT], F32)
        st1 = sb("st1", [128, MT], F32)
        r20 = sb("r20", [128, MT], BF16)
        r21 = sb("r21", [128, MT], BF16)
        xf = sb("xf", [128, 8, 128], F32)
        rowa = sb("rowa", [1, MT], F32)
        rowb = sb("rowb", [1, MT], F32)

        lncols = sb("lncols", [128, 8], F32)
        gcols = sb("gcols", [128, 8], F32)
        mw = sb("mw", [128, NTC, E], F32)
        lam = sb("lam", [128, NDC], F32)
        betas = sb("betas", [128, NDC], F32)
        gws = sb("gws", [128, NDC, E], F32)
        b1s = sb("b1s", [128, E, DFF // 128], F32)
        gts = sb("gts", [128, NDC], F32)
        bts = sb("bts", [128, NDC], F32)
        pstate = sb("pstate", [128, NDC], F32)
        b2s0 = sb("b2s0", [1, D], BF16)
        b2s1 = sb("b2s1", [1, D], BF16)
        ocb = sb("ocb", [128, 1], BF16)
        orb = sb("orb", [1, 128], BF16)
        orf = sb("orf", [1, 128], F32)
        idf = sb("idf", [128, 128], F32)
        epsc = sb("epsc", [128, 1], F32)
        idxc = [sb(f"idxc{i}", [128, NGC], I32) for i in range(4)]
        mgS = [sb(f"mg{i}", [128, NGC, E], F32) for i in range(4)]
        ysb = sb("ysb", [128, 2, NGC, D], BF16)
        mkf = sb("mkf", [128, NTC], F32)
        mkb = sb("mkb", [128, NTC], BF16)
        posw = sb("posw", [128, NTC], F32)
        t1w = sb("t1w", [128, NTC], F32)
        posi = sb("posi", [128, NTC], I32)
        crow = sb("crow", [1, 2 * NTC], F32)
        triSs = sb("triSs", [128, 128], BF16)
        padr = sb("padr", [1, CAP], I32)
        ytb = [sb("ytb0", [128, D], F32), sb("ytb1", [128, D], F32)]
        iotas = sb("iotas", [128, NTC], I32)
        pt0 = ps("pt0", [128, 1024], F32)
        pt1 = ps("pt1", [128, 1024], F32)
        pt2 = ps("pt2", [128, 1024], F32)
        pt3 = ps("pt3", [128, 1024], F32)
        sems = {}
        for e in ENGINES:
            sems[e] = sm(f"s_{e}")
        for qn in ("qw", "qx", "qo"):
            for i in range(NQSEM):
                sems[(qn, i)] = sm(f"s_{qn}{i}")
        block = _stk.enter_context(nc.Block())
        C = [c0, c1, c2, c3, c4, c5]
        W1B = [w1b0, w1b1]
        XT = [xt0, xt1]
        ST = [st0, st1]
        R2 = [r20, r21]
        B2S = [b2s0, b2s1]
        PT = [pt0, pt1, pt2, pt3]
        p = Prog()
        POOLREGS = {}

        def _pl(eng):
            POOLREGS["cap"] = eng.to_reg(CAP - 1)
            POOLREGS["mt"] = eng.to_reg(MT - 1)
        p.pool_prologue = _pl

        def pview(b, rows, cols, coloff=0):
            t, off = PT[b // 2], (b % 2) * 512
            return t[0:rows, off + coloff: off + coloff + cols]

        PBK = [("pb", i) for i in range(8)]

        def ck(ci, slot):
            return ("c", ci, slot)

        CALL = lambda ci: [ck(ci, s) for s in range(NDC)]

        def dbg_probe(k):
            pass

        p.dma("sp", lambda e: e.dma_start(out=betas[:], in_=d_betaT[:]), w=[("betas",)])
        p.dma("sp", lambda e: e.dma_start(out=gws[:], in_=d_gwT[:]), w=[("gws",)])
        p.dma("sp", lambda e: e.dma_start(out=b1s[:], in_=d_b1T[:]), w=[("b1s",)])
        p.dma("sp", lambda e: e.dma_start(out=gts[:], in_=d_gT[:]), w=[("gts",)])
        p.dma("sp", lambda e: e.dma_start(out=bts[:], in_=d_bT[:]), w=[("bts",)])
        p.dma("sp", lambda e: e.dma_start(out=ocb[:], in_=d_ocb[:]), w=[("ocb",)])
        p.dma("sp", lambda e: e.dma_start(out=orb[:], in_=d_orb[:]), w=[("orb",)])
        p.dma("sp", lambda e: e.dma_start(out=orf[:], in_=d_orf[:]), w=[("orf",)])
        p.dma("sp", lambda e: e.dma_start(out=idf[:], in_=d_idf[:]), w=[("idf",)])
        p.dma("sp", lambda e: e.dma_start(out=triSs[:], in_=d_tri[:]), w=[("tri",)])
        p.dma("sp", lambda e: e.dma_start(out=iotas[:], in_=d_iotaI[:]), w=[("iotas",)])
        for _i in range(4):
            p.dve(lambda e, _i=_i: e.memset(mgS[_i][:], 0.0), w=[("mg", _i)])
        p.dma("sp", lambda e: e.dma_start(out=padr[:], in_=d_padrow[:]), w=[("padr",)])
        p.act(lambda e: e.activation(lam[:], betas[:], AF.Exp, scale=NLN2),
              r=[("betas",)], w=[("lam",)])
        p.dve(lambda e: e.tensor_scalar(lam[:], lam[:], -1.0, 1.0, ALU.mult, ALU.add),
              r=[("lam",)], w=[("lam",)])
        p.dve(lambda e: e.tensor_scalar(lam[:], lam[:], TINY, 1.0, ALU.max, ALU.min),
              r=[("lam",)], w=[("lam",)])
        p.dve(lambda e: e.memset(pstate[:], 0.0),
              w=[("pst", cc) for cc in range(NDC)])
        p.dve(lambda e: e.memset(epsc[:], LN_EPS), w=[("epsc",)])

        tr_banks = (4, 5)
        gate_banks = (6, 7)

        def ln_chunk(x_dram, tci, par, boundary, mtv):
            xt = XT[par]
            kx = ("xt", par)
            tc = tci % NTC
            if LN_LIMIT is not None:
                if ln_ctr[0] >= LN_LIMIT:
                    return
                ln_ctr[0] += 1
            sub = 99 if LN_SUB is None else LN_SUB
            negmu = lncols[:, par: par + 1]
            sumsq = lncols[:, 2 + par: 3 + par]
            lnv = lncols[:, 4 + par: 5 + par]
            rstd = lncols[:, 6 + par: 7 + par]
            kc = ("lncols", par)
            p.dma("sp", lambda e: e.dma_start(
                out=xt[:], in_=x_dram[tci * 128:(tci + 1) * 128, :]), w=[kx])
            if sub < 2:
                return
            if sub < 3:
                return
            p.dve(lambda e: e.reduce_sum(negmu[:], xt[:], axis=AX.X, negate=True),
                  r=[kx], w=[kc])
            p.dve(lambda e: e.tensor_scalar(negmu[:], negmu[:], 1.0 / D, None, ALU.mult),
                  r=[kc], w=[kc])
            p.dve(lambda e: e.tensor_scalar(xt[:], xt[:], negmu[:], None, ALU.add),
                  r=[kx, kc], w=[kx])
            p.dve(lambda e: e.scalar_tensor_tensor(
                R2[par][:], xt[:], 1.0, xt[:], ALU.mult, ALU.mult, accum_out=sumsq[:]),
                r=[kx], w=[("r2", par), kc])
            if sub < 4:
                return
            p.act(lambda e: e.activation(lnv[:], sumsq[:], AF.Ln, bias=epsc[:], scale=1.0 / D),
                  r=[kc, ("epsc",)], w=[kc])
            p.act(lambda e: e.activation(rstd[:], lnv[:], AF.Exp, scale=-0.5),
                  r=[kc], w=[kc])
            p.act(lambda e: e.activation(xt[:], xt[:], AF.Identity, scale=rstd[:]),
                  r=[kx, kc], w=[kx])
            if not boundary:
                p.dma("sp", lambda e, tc=tc, mtv=mtv: e.dma_start(
                    out=d_xnd[(mtv * NTC + tc) * 128:(mtv * NTC + tc + 1) * 128, :],
                    in_=xt[:]),
                    r=[kx], w=[("xnd", mtv, tc)])
            if DBG and not boundary:
                p.dma("act", lambda e: e.dma_start(
                    out=d_dbg_xn[(mtv * NTC + tc) * 128:(mtv * NTC + tc + 1) * 128, :],
                    in_=xt[:]), r=[kx])
            if sub < 5:
                return
            for dc in range(NDC):
                tb = tr_banks[dc % 2]
                tpv = pview(tb, 128, 128)
                p.pe(lambda e, dc=dc, tpv=tpv: e.transpose(
                    tpv, xt[:, dc * 128:(dc + 1) * 128], idf[:]),
                    r=[kx, ("idf",)], w=[PBK[tb]])
                if boundary:
                    p.act(lambda e, dc=dc, tpv=tpv, tc=tc: e.activation(
                        xnt[:, dc, tc * 128:(tc + 1) * 128], tpv, AF.Identity),
                        r=[PBK[tb]], w=[("xnt", dc, tc)])
                else:
                    p.act(lambda e, dc=dc, tpv=tpv: e.activation(
                        xf[:, dc, :], tpv, AF.Identity),
                        r=[PBK[tb]], w=[("xf", dc)])
                    p.dve(lambda e, dc=dc, tc=tc: e.tensor_copy(
                        xnt[:, dc, tc * 128:(tc + 1) * 128], xf[:, dc, :]),
                        r=[("xf", dc)], w=[("xnt", dc, tc)])
            if sub < 6 or boundary:
                return
            if True:
                gb = gate_banks[tc % 2]
                gpv = pview(gb, 128, E)
                for dc in range(NDC):
                    p.pe(lambda e, dc=dc, gpv=gpv: e.matmul(
                        gpv, xf[:, dc, :], gws[:, dc, :],
                        start=(dc == 0), stop=(dc == NDC - 1)),
                        r=[("xf", dc), ("gws",)], w=[PBK[gb]])
                gpar = tc % 2
                nmx = gcols[:, gpar: gpar + 1]
                mxc = gcols[:, 2 + gpar: 3 + gpar]
                den = gcols[:, 4 + gpar: 5 + gpar]
                rden = gcols[:, 6 + gpar: 7 + gpar]
                kg = ("gcols", gpar)
                kmw = ("mw", tc)
                p.dve(lambda e, gpv=gpv: e.reduce_max(nmx[:], gpv, axis=AX.X, negate=True),
                      r=[PBK[gb]], w=[kg])
                p.act(lambda e, gpv=gpv, tc=tc: e.activation(
                    mw[:, tc, :], gpv, AF.Exp, bias=nmx[:]),
                    r=[PBK[gb], kg], w=[kmw])
                p.dve(lambda e, tc=tc: e.reduce_sum(den[:], mw[:, tc, :], axis=AX.X),
                      r=[kmw], w=[kg])
                p.dve(lambda e: e.reciprocal(rden[:], den[:]), r=[kg], w=[kg])
                p.dve(lambda e: e.tensor_scalar(mxc[:], nmx[:], -1.0, None, ALU.mult),
                      r=[kg], w=[kg])
                p.dve(lambda e, gpv=gpv, tc=tc: e.tensor_scalar(
                    mw[:, tc, :], gpv, mxc[:], rden[:], ALU.is_ge, ALU.mult),
                    r=[PBK[gb], kg], w=[kmw])
                p.dma("act", lambda e, tc=tc, mtv=mtv: e.dma_start(
                    out=d_mwd[(mtv * NTC + tc) * 128:(mtv * NTC + tc + 1) * 128, :],
                    in_=mw[:, tc, :]),
                    r=[kmw], w=[("mwd", mtv, tc)])
                if DBG:
                    p.dma("act", lambda e, tc=tc: e.dma_start(
                        out=d_dbg_mw[(mtv * NTC + tc) * 128:(mtv * NTC + tc + 1) * 128, :],
                        in_=mw[:, tc, :]), r=[kmw])

        proj_ctr = [0]
        ln_ctr = [0]

        def projection(dsti, wbi, silu=False):
            dst = C[dsti]
            wbuf = C[wbi]
            for ic in range(NDC):
                for half in range(2):
                    bk = proj_ctr[0] % 4
                    proj_ctr[0] += 1
                    ppv = pview(bk, 128, 512)
                    rk0 = [ck(wbi, dc) for dc in range(NDC)] + \
                          [("xnt", dc, t) for dc in range(NDC)
                           for t in range(half * 4, half * 4 + 4)]
                    for dc in range(NDC):
                        p.pe(lambda e, ic=ic, half=half, dc=dc, ppv=ppv, wbuf=wbuf: e.matmul(
                            ppv, wbuf[:, dc, ic * 128:(ic + 1) * 128],
                            xnt[:, dc, half * 512:(half + 1) * 512],
                            start=(dc == 0), stop=(dc == NDC - 1)),
                            r=rk0 if dc == 0 else [ck(wbi, dc)], w=[PBK[bk]])
                    if silu:
                        p.act(lambda e, ic=ic, half=half, ppv=ppv, dst=dst: e.activation(
                            dst[:, ic, half * 512:(half + 1) * 512], ppv, AF.Silu),
                            r=[PBK[bk]], w=[ck(dsti, ic)])
                    else:
                        p.dve(lambda e, ic=ic, half=half, ppv=ppv, dst=dst: e.tensor_copy(
                            dst[:, ic, half * 512:(half + 1) * 512], ppv),
                            r=[PBK[bk]], w=[ck(dsti, ic)])

        def w_dma(dsti, src):
            p.dma("pool", lambda e, dsti=dsti: e.dma_start(
                out=C[dsti][:], in_=src.rearrange("(dc p) i -> p dc i", p=128)),
                w=CALL(dsti))

        def scan_chunk(cc, par, vti):
            ks = ("st", par)
            p.dve(lambda e, cc=cc, par=par, vti=vti: e.tensor_tensor_scan(
                ST[par][:], lam[:, cc:cc + 1].broadcast_to([128, MT]),
                C[vti][:, cc, :], pstate[:, cc:cc + 1], ALU.mult, ALU.add),
                r=[ck(vti, cc), ("lam",), ("pst", cc)], w=[ks])
            p.dve(lambda e, cc=cc, par=par: e.tensor_copy(
                pstate[:, cc:cc + 1], ST[par][:, MT - 1:MT]),
                r=[ks], w=[("pst", cc)])

        if "boundary" in PHASES:
            w_dma(4, d_wT["wvT"])
            for pmt in range(2):
                for tc in range(NTC):
                    ln_chunk(d_xp, pmt * NTC + tc, tc % 2, True, 0)
                projection(1, 4)
                for cc in range(NDC):
                    scan_chunk(cc, cc % 2, 1)

        for mtv in range(2):
            mtbase = mtv * MT
            if "ln" not in PHASES:
                break

            for tc in range(NTC):
                ln_chunk(d_x, mtv * NTC + tc, tc % 2, False, mtv)

            if mtv == 0:
                dbg_probe(0)
            if "qvg" not in PHASES:
                continue
            if mtv == 1 or "boundary" not in PHASES:
                w_dma(4, d_wT["wvT"])
            projection(1, 4)
            w_dma(5, d_wT["wqT"])
            projection(0, 5)
            w_dma(4, d_wT["wgT"])
            projection(2, 4, silu=True)
            w_dma(5, d_wT["woT"])
            if "ffn" not in PHASES and "route" not in PHASES and "gath" not in PHASES:
                continue
            for ei in range(E):
                kmk = ("mkf",)
                p.dve(lambda e, ei=ei: e.tensor_scalar(
                    mkf[:], mw[:, :, ei], 0.0, None, ALU.is_gt),
                    r=[("mw", t) for t in range(NTC)], w=[kmk])
                p.dve(lambda e: e.tensor_copy(mkb[:], mkf[:]), r=[kmk], w=[("mkb",)])
                p.pe(lambda e: e.matmul(pview(0, 128, NTC), triSs[:], mkb[:],
                                        start=True, stop=True),
                     r=[("mkb",), ("tri",)], w=[PBK[0]])
                p.pe(lambda e: e.matmul(pview(1, 1, NTC), ocb[:], mkb[:],
                                        start=True, stop=True),
                     r=[("mkb",), ("ocb",)], w=[PBK[1]])
                p.dve(lambda e: e.tensor_tensor_scan(
                    crow[0:1, 0:NTC], orf[0:1, 0:NTC], pview(1, 1, NTC), 0.0,
                    ALU.mult, ALU.add), r=[PBK[1], ("orf",)], w=[("crow",)])
                p.dve(lambda e: e.tensor_tensor(
                    crow[0:1, NTC:2 * NTC], crow[0:1, 0:NTC], pview(1, 1, NTC),
                    ALU.subtract), r=[("crow",), PBK[1]], w=[("crow",)])
                p.pe(lambda e: e.matmul(pview(2, 128, NTC), orf[:],
                                        crow[0:1, NTC:2 * NTC], start=True, stop=True),
                     r=[("crow",), ("orf",)], w=[PBK[2]])
                p.dve(lambda e: e.tensor_copy(posw[:], pview(0, 128, NTC)),
                      r=[PBK[0]], w=[("posw",)])
                p.dve(lambda e: e.tensor_tensor(posw[:], posw[:], pview(2, 128, NTC),
                                                ALU.add),
                      r=[("posw",), PBK[2]], w=[("posw",)])
                p.dve(lambda e: e.tensor_scalar(t1w[:], mkf[:], -BIGPOS, BIGPOS,
                                                ALU.mult, ALU.add),
                      r=[kmk], w=[("t1w",)])
                p.dve(lambda e: e.tensor_tensor(posw[:], posw[:], t1w[:], ALU.add),
                      r=[("posw",), ("t1w",)], w=[("posw",)])
                p.dve(lambda e: e.tensor_copy(posi[:], posw[:]),
                      r=[("posw",)], w=[("posi",)])
                gr = mtv * E + ei
                p.dma("pool", lambda e, gr=gr: e.dma_start(
                    out=d_gl[gr:gr + 1, :], in_=padr[:]),
                    r=[("padr",)], w=[("gl", gr)])
                for tc in range(NTC):
                    p.dma("pool", lambda e, gr=gr, tc=tc: e.indirect_dma_start(
                        out=d_gl.rearrange("e c -> (e c)").unsqueeze(1),
                        out_offset=IndirectOffsetOnAxis(posi[:, tc:tc + 1], 0),
                        in_=iotas[:, tc:tc + 1], in_offset=None,
                        element_offset=gr * CAP,
                        bounds_check=POOLREGS["cap"], oob_is_err=False),
                        r=[("posi",), ("iotas",)], w=[("gl", gr)])
            if mtv == 0:
                dbg_probe(1)
            if "scan" not in PHASES:
                continue

            for cc in range(NDC):
                par = cc % 2
                scan_chunk(cc, par, 1)
                p.dve(lambda e, cc=cc, par=par: e.tensor_tensor(
                    C[1][:, cc, :], ST[par][:], C[0][:, cc, :], ALU.mult),
                    r=[("st", par), ck(0, cc)], w=[ck(1, cc)])
                p.dve(lambda e, cc=cc, par=par: e.scalar_tensor_tensor(
                    R2[par][:], C[1][:, cc, :], 1.0, C[1][:, cc, :], ALU.mult, ALU.mult),
                    r=[ck(1, cc)], w=[("r2", par)])
                for jt in range(2):
                    mb = 4 + jt
                    sb = 6 + jt
                    p.pe(lambda e, cc=cc, jt=jt, mb=mb: e.matmul(
                        pview(mb, 1, 512), ocb[:], C[1][:, cc, jt * 512:(jt + 1) * 512],
                        start=(cc == 0), stop=(cc == NDC - 1)),
                        r=[ck(1, cc), ("ocb",)], w=[PBK[mb]])
                    p.pe(lambda e, cc=cc, jt=jt, par=par, sb=sb: e.matmul(
                        pview(sb, 1, 512), ocb[:], R2[par][:, jt * 512:(jt + 1) * 512],
                        start=(cc == 0), stop=(cc == NDC - 1)),
                        r=[("r2", par), ("ocb",)], w=[PBK[sb]])

            if mtv == 0:
                dbg_probe(2)
            if "norm" not in PHASES:
                continue
            for jt in range(2):
                mb, sb = 4 + jt, 6 + jt
                ra = rowa[0:1, jt * 512:(jt + 1) * 512]
                rb = rowb[0:1, jt * 512:(jt + 1) * 512]
                p.dve(lambda e, mb=mb, ra=ra: e.tensor_scalar(
                    ra, pview(mb, 1, 512), -1.0 / D, None, ALU.mult),
                    r=[PBK[mb]], w=[("rowa", jt)])
                p.dve(lambda e, ra=ra, rb=rb: e.tensor_tensor(rb, ra, ra, ALU.mult),
                      r=[("rowa", jt)], w=[("rowb", jt)])
                p.dve(lambda e, sb=sb, rb=rb: e.scalar_tensor_tensor(
                    rb, pview(sb, 1, 512), 1.0 / D, rb, ALU.mult, ALU.subtract),
                    r=[PBK[sb], ("rowb", jt)], w=[("rowb", jt)])
                p.act(lambda e, rb=rb: e.activation(rb, rb, AF.Ln, bias=epsc[0:1, 0:1]),
                      r=[("rowb", jt), ("epsc",)], w=[("rowb", jt)])
                p.act(lambda e, rb=rb: e.activation(rb, rb, AF.Exp, scale=-0.5),
                      r=[("rowb", jt)], w=[("rowb", jt)])
                p.pe(lambda e, jt=jt, ra=ra: e.matmul(
                    pview(4 + jt, 128, 512), orf[:], ra, start=True, stop=True),
                    r=[("rowa", jt), ("orf",)], w=[PBK[4 + jt]])
                p.pe(lambda e, jt=jt, rb=rb: e.matmul(
                    pview(6 + jt, 128, 512), orf[:], rb, start=True, stop=True),
                    r=[("rowb", jt), ("orf",)], w=[PBK[6 + jt]])

            for cc in range(NDC):
                par = cc % 2
                sv = ST[par]
                ks = ("st", par)
                p.dve(lambda e, cc=cc, sv=sv: e.tensor_tensor(
                    sv[:], C[1][:, cc, :], pt2[:, :], ALU.add),
                    r=[ck(1, cc), PBK[4], PBK[5]], w=[ks])
                p.dve(lambda e, sv=sv: e.tensor_tensor(sv[:], sv[:], pt3[:, :], ALU.mult),
                      r=[ks, PBK[6], PBK[7]], w=[ks])
                p.dve(lambda e, cc=cc, sv=sv: e.tensor_scalar(
                    sv[:], sv[:], gts[:, cc:cc + 1], bts[:, cc:cc + 1], ALU.mult, ALU.add),
                    r=[ks, ("gts",), ("bts",)], w=[ks])
                p.dve(lambda e, cc=cc, sv=sv: e.tensor_tensor(
                    C[3][:, cc, :], sv[:], C[2][:, cc, :], ALU.mult),
                    r=[ks, ck(2, cc)], w=[ck(3, cc)])

            if mtv == 0:
                dbg_probe(3)
            if "wo" not in PHASES:
                continue
            for tc in range(NTC):
                ybank = (tc % 2) * 2
                ypt = PT[tc % 2]
                ybk = [PBK[ybank], PBK[ybank + 1]]
                for jt in range(2):
                    for ic in range(NDC):
                        p.pe(lambda e, tc=tc, jt=jt, ic=ic, ypt=ypt: e.matmul(
                            ypt[:, jt * 512:(jt + 1) * 512],
                            C[3][:, ic, tc * 128:(tc + 1) * 128],
                            C[5][:, ic, jt * 512:(jt + 1) * 512],
                            start=(ic == 0), stop=(ic == NDC - 1)),
                            r=[ck(3, ic), ck(5, ic)], w=[ybk[jt]])
                par = tc % 2
                kxx = ("xt", par)
                kss = ("st", par)
                p.dma("sp", lambda e, tc=tc, mtbase=mtbase, par=par: e.dma_start(
                    out=XT[par][:], in_=d_x[mtbase + tc * 128: mtbase + (tc + 1) * 128, :]),
                    w=[kxx])
                p.dve(lambda e, tc=tc, ypt=ypt, par=par: e.tensor_tensor(
                    ST[par][:], XT[par][:], ypt[:, :], ALU.add),
                    r=ybk + [kxx], w=[kss])
                p.dma("act", lambda e, tc=tc, mtbase=mtbase, par=par: e.dma_start(
                    out=d_out[mtbase + tc * 128: mtbase + (tc + 1) * 128, :],
                    in_=ST[par][:]), r=[kss], w=[("outd", mtv, tc)])
                if DBG:
                    p.dma("act", lambda e, tc=tc, mtbase=mtbase, par=par: e.dma_start(
                        out=d_dbg_attn[(mtbase + tc * 128):(mtbase + (tc + 1) * 128), :],
                        in_=ST[par][:]), r=[kss])

            if DBG and mtv == 0:
                for i in (1, 2, 3, 5):
                    p.dma("pool", lambda e, i=i: e.dma_start(
                        out=d_dbg_c[i][:], in_=C[i][:]),
                        r=[ck(i, s) for s in range(NDC)])
                p.dma("pool", lambda e: e.dma_start(out=d_dbg_rows[0:1, :], in_=rowa[:]),
                      r=[("rowa", 0), ("rowa", 1)])
                p.dma("pool", lambda e: e.dma_start(out=d_dbg_rows[1:2, :], in_=rowb[:]),
                      r=[("rowb", 0), ("rowb", 1)])


        if "ffn" in PHASES or "gath" in PHASES:
            XNTKEYS = [("xnt", dc, t) for dc in range(NDC) for t in range(NTC)]

            def gather_expert(ei, mtv):
                slot = 2 * (ei % 2) + mtv
                mtbase = mtv * MT
                gr = mtv * E + ei
                kxg = ("xgt", slot)
                for gc in range(NGC):
                    p.dma("pool", lambda e, gr=gr, gc=gc, slot=slot: e.dma_start(
                        out=idxc[slot][:, gc:gc + 1],
                        in_=d_gl[gr, gc * 128:(gc + 1) * 128].unsqueeze(1)),
                        r=[("gl", gr)], w=[("idx", slot, gc)])
                for gc in range(NGC):
                    par = gc % 2
                    kx = ("xt", par)
                    p.dma("pool", lambda e, gc=gc, slot=slot, par=par,
                          mtbase=mtbase: e.indirect_dma_start(
                        out=XT[par][:], out_offset=None,
                        in_=d_xnd[:],
                        in_offset=IndirectOffsetOnAxis(idxc[slot][:, gc:gc + 1], 0),
                        element_offset=mtbase * D,
                        bounds_check=POOLREGS["mt"], oob_is_err=False),
                        r=[("idx", slot, gc)]
                        + [("xnd", m, t) for m in range(2) for t in range(NTC)],
                        w=[kx])
                    p.dma("pool", lambda e, gc=gc, slot=slot,
                          mtbase=mtbase: e.indirect_dma_start(
                        out=mgS[slot][:, gc, :], out_offset=None,
                        in_=d_mwd[:],
                        in_offset=IndirectOffsetOnAxis(idxc[slot][:, gc:gc + 1], 0),
                        element_offset=mtbase * E,
                        bounds_check=POOLREGS["mt"], oob_is_err=False),
                        r=[("idx", slot, gc)]
                        + [("mwd", m, t) for m in range(2) for t in range(NTC)],
                        w=[("mg", slot)])
                    for dc in range(NDC):
                        tb = 6 + dc % 2
                        p.pe(lambda e, gc=gc, dc=dc, par=par, tb=tb: e.transpose(
                            pview(tb, 128, 128), XT[par][:, dc * 128:(dc + 1) * 128],
                            idf[:]),
                            r=[("xt", par), ("idf",)], w=[PBK[tb]])
                        p.act(lambda e, gc=gc, dc=dc, slot=slot, tb=tb: e.activation(
                            XGT(slot)[:, dc, gc * 128:(gc + 1) * 128],
                            pview(tb, 128, 128), AF.Identity),
                            r=[PBK[tb]], w=[kxg] + (XNTKEYS if True else []))

            def h1v(mtv, fc16, cols=None):
                ch = C[mtv]
                sl = fc16 // 2
                off = (fc16 % 2) * CAP
                if cols is None:
                    return ch[:, sl, off:off + CAP]
                return ch[:, sl, off + cols[0]:off + cols[1]]

            def h1key(mtv, fc16):
                return ck(mtv, fc16 // 2)

            stages = [(ei, h) for ei in range(E) for h in range(2)]
            w18ctr = [0]
            stage_w1 = {}

            def stage_w2_dmas(si):
                ei, h = stages[si]
                if h == 0:
                    b2par = ei % 2
                    p.dma("pool", lambda e, ei=ei, b2par=b2par: e.dma_start(
                        out=B2S[b2par][:], in_=d_b2[ei:ei + 1, :]), w=[("b2s", b2par)])
                w2pair = (2, 3) if si % 2 == 0 else (4, 5)
                for ci in range(2):
                    cdst = C[w2pair[ci]]
                    p.dma("pool", lambda e, ei=ei, h=h, ci=ci, cdst=cdst: e.dma_start(
                        out=cdst[:],
                        in_=d_w2[ei][h * 2048 + ci * 1024: h * 2048 + (ci + 1) * 1024, :]
                        .rearrange("(fc p) c -> p fc c", p=128)),
                        w=CALL(w2pair[ci]))

            def stage_w1_dmas(si, ks):
                ei, h = stages[si]
                lst = stage_w1.setdefault(si, {})
                for k8 in ks:
                    wpar = w18ctr[0] % 2
                    w18ctr[0] += 1
                    wb = W1B[wpar]
                    wbk = ("w1b", wpar)
                    f0 = h * 2048 + k8 * 512
                    p.dma("pool", lambda e, ei=ei, f0=f0, wb=wb: e.dma_start(
                        out=wb[:],
                        in_=d_w1[ei][:, f0:f0 + 512].rearrange("(dc p) f -> p dc f", p=128)),
                        w=[wbk])
                    lst[k8] = (wb, wbk)

            gather_expert(0, 0)
            gather_expert(0, 1)
            if "ffn" in PHASES:
                stage_w2_dmas(0)
                stage_w1_dmas(0, (0, 1))
                nstages = len(stages) if FFN_STAGES is None else min(FFN_STAGES, len(stages))
                for si in range(nstages):
                    ei, h = stages[si]
                    b2par = ei % 2
                    w2pair = (2, 3) if si % 2 == 0 else (4, 5)
                    for k8 in range(4):
                        if k8 == 2:
                            stage_w1_dmas(si, (2, 3))
                        wb, wbk = stage_w1[si][k8]
                        for fcl in range(4):
                            fc16 = k8 * 4 + fcl
                            fc32 = h * 16 + fc16
                            for mtv in range(2):
                                slot = 2 * (ei % 2) + mtv
                                hb = 6 + (fcl * 2 + mtv) % 2
                                hpv = pview(hb, 128, CAP)
                                for dc in range(NDC):
                                    p.pe(lambda e, fcl=fcl, dc=dc, wb=wb, hpv=hpv,
                                         slot=slot: e.matmul(
                                             hpv, wb[:, dc, fcl * 128:(fcl + 1) * 128],
                                             XGT(slot)[:, dc, :],
                                             start=(dc == 0), stop=(dc == NDC - 1)),
                                        r=[wbk, ("xgt", slot)], w=[PBK[hb]])
                                p.act(lambda e, mtv=mtv, fc16=fc16, fc32=fc32,
                                      hpv=hpv, ei=ei: e.activation(
                                          h1v(mtv, fc16), hpv, AF.Relu,
                                          bias=b1s[:, ei, fc32:fc32 + 1]),
                                      r=[PBK[hb], ("b1s",)], w=[h1key(mtv, fc16)])
                    if h == 0 and ei + 1 < E and (FFN_STAGES is None or si + 2 < nstages):
                        gather_expert(ei + 1, 0)
                        gather_expert(ei + 1, 1)
                    if si + 1 < nstages:
                        stage_w1_dmas(si + 1, (0, 1))
                        stage_w2_dmas(si + 1)
                    for mtv in range(2):
                        for gc in range(NGC):
                            gi = mtv * NGC + gc
                            ypt = PT[gi % 2]
                            ybk = [PBK[(gi % 2) * 2], PBK[(gi % 2) * 2 + 1]]
                            for jt in range(2):
                                p.pe(lambda e, jt=jt, ypt=ypt, b2par=b2par: e.matmul(
                                    ypt[:, jt * 512:(jt + 1) * 512], orb[:],
                                    B2S[b2par][0:1, jt * 512:(jt + 1) * 512],
                                    start=True, stop=False),
                                    r=[("b2s", b2par), ("orb",)], w=[ybk[jt]])
                            for fc16 in range(16):
                                w2i = w2pair[fc16 // 8]
                                for jt in range(2):
                                    p.pe(lambda e, mtv=mtv, fc16=fc16, gc=gc, jt=jt,
                                         ypt=ypt, w2i=w2i: e.matmul(
                                             ypt[:, jt * 512:(jt + 1) * 512],
                                             h1v(mtv, fc16, (gc * 128, (gc + 1) * 128)),
                                             C[w2i][:, fc16 % 8, jt * 512:(jt + 1) * 512],
                                             start=False, stop=(fc16 == 15)),
                                        r=[h1key(mtv, fc16), ck(w2i, fc16 % 8)],
                                        w=[ybk[jt]])
                            ky = ("ysb", mtv, gc)
                            if h == 0:
                                p.dve(lambda e, mtv=mtv, gc=gc, ypt=ypt: e.tensor_copy(
                                    ysb[:, mtv, gc, :], ypt[:, :]),
                                    r=ybk, w=[ky])
                            else:
                                p.dve(lambda e, mtv=mtv, gc=gc, ypt=ypt:
                                      e.scalar_tensor_tensor(
                                          ysb[:, mtv, gc, :], ypt[:, :], 1.0,
                                          ysb[:, mtv, gc, :], ALU.mult, ALU.add),
                                    r=ybk + [ky], w=[ky])
                    if h == 1:
                        for mtv in range(2):
                            slot = 2 * (ei % 2) + mtv
                            mtbase = mtv * MT
                            for gc in range(NGC):
                                gi = mtv * NGC + gc
                                ky = ("ysb", mtv, gc)
                                yt = ST[gi % 2]
                                kyt = ("st", gi % 2)
                                kyb = ("ytb", gi % 2)
                                p.act(lambda e, mtv=mtv, gc=gc, slot=slot, ei=ei,
                                      yt=yt: e.activation(
                                          yt[:], ysb[:, mtv, gc, :], AF.Identity,
                                          scale=mgS[slot][:, gc, ei:ei + 1]),
                                      r=[ky, ("mg", slot)], w=[kyt])
                                p.dma("pool", lambda e, gc=gc, slot=slot,
                                      mtbase=mtbase, gi=gi: e.indirect_dma_start(
                                          out=ytb[gi % 2][:], out_offset=None,
                                          in_=d_out[:],
                                          in_offset=IndirectOffsetOnAxis(
                                              idxc[slot][:, gc:gc + 1], 0),
                                          element_offset=mtbase * D,
                                          bounds_check=POOLREGS["mt"], oob_is_err=False),
                                      r=[("idx", slot, gc)]
                                      + [("outd", m, t) for m in range(2)
                                         for t in range(NTC)],
                                      w=[kyb])
                                p.dve(lambda e, gi=gi, yt=yt: e.tensor_tensor(
                                    yt[:], yt[:], ytb[gi % 2][:], ALU.add),
                                    r=[kyt, kyb], w=[kyt])
                                p.dma("pool", lambda e, gc=gc, slot=slot, yt=yt,
                                      mtbase=mtbase, gi=gi: e.indirect_dma_start(
                                          out=d_out[:],
                                          out_offset=IndirectOffsetOnAxis(
                                              idxc[slot][:, gc:gc + 1], 0),
                                          in_=yt[:], in_offset=None,
                                          element_offset=mtbase * D,
                                          bounds_check=POOLREGS["mt"], oob_is_err=False),
                                      r=[kyt, ("idx", slot, gc)],
                                      w=[("outs", slot, gc)])

        p.finalize(block, sems)
    return nc


def kernel(L0_attn_x, L0_attn_Wq, L0_attn_Wv, L0_attn_Wo, L0_attn_Wg,
           L0_attn_beta, L0_attn_ln_gamma, L0_attn_ln_beta,
           L0_ffn_gate_w, L0_ffn_w1, L0_ffn_b1, L0_ffn_w2, L0_ffn_b2):
    f32 = np.float32
    x = np.asarray(L0_attn_x, dtype=f32)
    common = {
        "wqT": np.ascontiguousarray(np.asarray(L0_attn_Wq, f32).T),
        "wvT": np.ascontiguousarray(np.asarray(L0_attn_Wv, f32).T),
        "wgT": np.ascontiguousarray(np.asarray(L0_attn_Wg, f32).T),
        "woT": np.ascontiguousarray(np.asarray(L0_attn_Wo, f32).T),
        "gwT": np.ascontiguousarray(
            np.asarray(L0_ffn_gate_w, f32).T.reshape(NDC, 128, E).transpose(1, 0, 2)),
        "betaT": np.ascontiguousarray(
            np.repeat(np.asarray(L0_attn_beta, f32), Dh).reshape(NDC, 128).T),
        "gT": np.ascontiguousarray(np.asarray(L0_attn_ln_gamma, f32).reshape(NDC, 128).T),
        "bT": np.ascontiguousarray(np.asarray(L0_attn_ln_beta, f32).reshape(NDC, 128).T),
        "w1": np.ascontiguousarray(np.asarray(L0_ffn_w1, f32)),
        "b1T": np.ascontiguousarray(
            np.asarray(L0_ffn_b1, f32).reshape(E, DFF // 128, 128).transpose(2, 0, 1)),
        "w2": np.ascontiguousarray(np.asarray(L0_ffn_w2, f32)),
        "b2": np.ascontiguousarray(np.asarray(L0_ffn_b2, f32)),
        "ones_col_bf": np.ones((128, 1), dtype=ml_dtypes.bfloat16),
        "ones_row_bf": np.ones((1, 128), dtype=ml_dtypes.bfloat16),
        "ones_row_f": np.ones((1, 128), dtype=f32),
        "iden_f": np.eye(128, dtype=f32),
        "triS": np.triu(np.ones((128, 128)), k=1).astype(ml_dtypes.bfloat16),
        "iotaI": np.ascontiguousarray(
            (np.arange(NTC)[None, :] * 128 + np.arange(128)[:, None]).astype(np.int32)),
        "padrow": np.full((1, CAP), PADIDX, dtype=np.int32),
    }
    zeros = np.zeros((TPC, D), dtype=f32)
    in_maps = []
    for core in range(8):
        b, half = core // 2, core % 2
        m = dict(common)
        m["x"] = np.ascontiguousarray(x[b, half * TPC:(half + 1) * TPC, :])
        m["xp"] = zeros if half == 0 else np.ascontiguousarray(x[b, 0:TPC, :])
        in_maps.append(m)

    nc = build()
    res = run_bass_kernel_spmd(nc, in_maps, core_ids=list(range(8)))
    out = np.empty((B, L, D), dtype=f32)
    for core in range(8):
        b, half = core // 2, core % 2
        out[b, half * TPC:(half + 1) * TPC, :] = res.results[core]["out"]
    if DBG:
        kernel.dbg = res.results
    return out
